# version 17 (final)
# baseline (speedup 1.0000x reference)
import contextlib

import numpy as np
import ml_dtypes

import concourse.bass as bass
import concourse.mybir as mybir
from concourse.bass_utils import run_bass_kernel_spmd

F32 = mybir.dt.float32
BF16 = mybir.dt.bfloat16
U8 = mybir.dt.uint8
I32 = mybir.dt.int32
AF = mybir.ActivationFunctionType
ALU = mybir.AluOpType
AX = mybir.AxisListType

NT = 4096
D = 1024
NH, DH, HP = 16, 64, 8
NG, QW = 8, 512
E, HH, GDIM = 8, 2048, 256
NKC = 32
LN_EPS = 1e-5

ENGINES = ("tensor", "vector", "scalar", "gpsimd", "sync")


class Plan:

    def __init__(self):
        self.q = {e: [] for e in ENGINES}
        self.cnt = {}
        self.lastw = {e: {} for e in ENGINES}

    def val(self, sem):
        return self.cnt.get(id(sem), 0)

    def emit(self, eng, fn, waits=(), inc=None, incby=1):
        w = []
        for sem, v in waits:
            if v is None or v <= 0:
                continue
            k = id(sem)
            if self.lastw[eng].get(k, -1) >= v:
                continue
            self.lastw[eng][k] = v
            w.append((sem, v))

        import traceback as _tb
        _site = "".join(_tb.format_stack(limit=3)[:-1])

        def closure(e, w=tuple(w), fn=fn, inc=inc, incby=incby, _site=_site):
            for sem, v in w:
                e.wait_ge(sem, v)
            try:
                ins = fn(e)
            except Exception:
                print("EMIT SITE:\n", _site)
                raise
            if inc is not None:
                ins.then_inc(inc, incby)

        self.q[eng].append(closure)
        if inc is not None:
            self.cnt[id(inc)] = self.cnt.get(id(inc), 0) + incby
        return self.val(inc) if inc is not None else None


class Arena:

    def __init__(self, nc, kib=206):
        self.h = nc.alloc_sbuf_tensor("arena", [128, kib * 1024], U8)
        self.base = 0
        self.cap = kib * 1024
        self.marks = []

    def alloc(self, dtype, shape_free, align=128):
        esz = {F32: 4, BF16: 2, U8: 1, mybir.dt.int32: 4}[dtype]
        n = 1
        for s in shape_free:
            n *= s
        off = (self.base + align - 1) // align * align
        self.base = off + n * esz
        assert self.base <= self.cap, f"SBUF arena overflow: {self.base} > {self.cap}"
        ap = self.h.ap()[:, off:off + n * esz].bitcast(dtype)
        if len(shape_free) > 1:
            pat = "p (" + " ".join(f"d{i}" for i in range(len(shape_free))) + ") -> p " + \
                  " ".join(f"d{i}" for i in range(len(shape_free)))
            ap = ap.rearrange(pat, **{f"d{i}": s for i, s in enumerate(shape_free)})
        return ap

    def mark(self):
        self.marks.append(self.base)

    def reset(self):
        self.base = self.marks.pop()


RSQ_MAGIC = 0x5F3759DF


def plan_rsqrt(p, S, out, x, rsq, waits=(), eps=0.0):
    w = out.shape[-1]
    xe = rsq[:, 0:w]
    t1 = rsq[:, 8:8 + w]
    t2 = rsq[:, 16:16 + w]
    yi = out.bitcast(I32)
    xei = xe.bitcast(I32)

    def step(fn):
        p.emit("vector", fn, waits=[(S["semV"], p.val(S["semV"]))], inc=S["semV"])

    p.emit("vector", lambda e: e.tensor_scalar(
        out=xe, in0=x, scalar1=eps, scalar2=None, op0=ALU.add),
        waits=list(waits) + [(S["semV"], p.val(S["semV"]))], inc=S["semV"])
    step(lambda e: e.tensor_scalar(out=yi, in0=xei, scalar1=1, scalar2=None,
                                   op0=ALU.arith_shift_right))
    step(lambda e: e.tensor_scalar(out=yi, in0=yi, scalar1=RSQ_MAGIC, scalar2=-1,
                                   op0=ALU.subtract, op1=ALU.mult))
    for _ in range(3):
        step(lambda e: e.tensor_tensor(out=t1, in0=out, in1=out, op=ALU.mult))
        step(lambda e: e.tensor_tensor(out=t2, in0=t1, in1=xe, op=ALU.mult))
        step(lambda e: e.tensor_scalar(out=t2, in0=t2, scalar1=-0.5, scalar2=1.5,
                                       op0=ALU.mult, op1=ALU.add))
        step(lambda e: e.tensor_tensor(out=out, in0=out, in1=t2, op=ALU.mult))
    return p.val(S["semV"])


def bf16_bank(bank):
    return bank[:, :].bitcast(BF16)


def build_program(upto="C", debug=False):
    nc = bass.Bass("TRN2", target_bir_lowering=False)

    x_h = nc.dram_tensor("x", [NT, D], F32, kind="ExternalInput").ap()
    wqkv_h = nc.dram_tensor("wqkv", [D, 3 * D], F32, kind="ExternalInput").ap()
    wg1_h = nc.dram_tensor("wg1", [D, GDIM], F32, kind="ExternalInput").ap()
    bg1_h = nc.dram_tensor("bg1", [GDIM], F32, kind="ExternalInput").ap()
    wg2_h = nc.dram_tensor("wg2", [GDIM, E], F32, kind="ExternalInput").ap()
    win_h = nc.dram_tensor("w_in", [E, D, HH], F32, kind="ExternalInput").ap()
    wout_h = nc.dram_tensor("w_out", [E, HH, D], F32, kind="ExternalInput").ap()
    identb_h = nc.dram_tensor("identb", [128, 128], BF16, kind="ExternalInput").ap()
    identf_h = nc.dram_tensor("identf", [128, 128], F32, kind="ExternalInput").ap()
    ones16_h = nc.dram_tensor("ones16", [128, 128], BF16, kind="ExternalInput").ap()
    onesf_h = nc.dram_tensor("onesf", [128, 128], F32, kind="ExternalInput").ap()
    masks_h = nc.dram_tensor("masks", [128, 4 * QW], BF16, kind="ExternalInput").ap()
    eiota_h = nc.dram_tensor("eiota", [128, 8], F32, kind="ExternalInput").ap()
    tokid_h = nc.dram_tensor("tokid", [128, 8], F32, kind="ExternalInput").ap()
    parange_h = nc.dram_tensor("parange", [128, 1], F32, kind="ExternalInput").ap()
    thr256_h = nc.dram_tensor("thr256", [128, 4], F32, kind="ExternalInput").ap()
    tilest_h = nc.dram_tensor("tilest", [128, 16], F32, kind="ExternalInput").ap()

    out_h = nc.dram_tensor("out", [NT, D], F32, kind="ExternalOutput").ap()

    skind = "ExternalOutput" if debug else "Internal"
    q_scr = nc.dram_tensor("q_scr", [HP, 128, NT], BF16, kind=skind).ap()
    k_scr = nc.dram_tensor("k_scr", [HP, 128, NT], BF16, kind=skind).ap()
    v_scr = nc.dram_tensor("v_scr", [NKC, 128, NH * 65], BF16, kind=skind).ap()
    cb_scr = nc.dram_tensor("cb_scr", [128, 128], F32, kind=skind).ap()
    dbgC = nc.dram_tensor("dbgC", [128, 128], F32, kind="ExternalOutput" if debug else "Internal").ap()
    xp_scr = nc.dram_tensor("xp_scr", [NT, D], F32, kind=skind).ap()
    xn2_scr = nc.dram_tensor("xn2_scr", [8, 128, NT], BF16, kind=skind).ap()
    wg1_scr = nc.dram_tensor("wg1_scr", [128, 8 * GDIM], BF16, kind="Internal").ap()
    wg2_scr = nc.dram_tensor("wg2_scr", [128, 2 * E], BF16, kind="Internal").ap()
    wi_scr = nc.dram_tensor("wi_scr", [E, 128, 8 * HH], BF16, kind="Internal").ap()
    wo_scr = nc.dram_tensor("wo_scr", [E, 128, 16 * D], BF16, kind="Internal").ap()
    xn2tm_scr = nc.dram_tensor("xn2tm_scr", [NT, D], BF16, kind=skind).ap()
    stream_scr = [nc.dram_tensor(f"stream_scr{i}", [3840], mybir.dt.int32,
                                 kind=skind).ap() for i in range(2)]
    gates_scr = [nc.dram_tensor(f"gates_scr{i}", [3840], F32,
                                kind=skind).ap() for i in range(2)]
    o_scr = [nc.dram_tensor(f"o_scr{i}", [3840, D], F32,
                            kind=skind).ap() for i in range(2)]

    p = Plan()

    with contextlib.ExitStack() as ctx:
        en = ctx.enter_context
        ar = Arena(nc, kib=206)
        sAp = en(nc.psum_tensor("sA", [128, 1024], F32))
        sBp = en(nc.psum_tensor("sB", [128, 1024], F32))
        sCp = en(nc.psum_tensor("sC", [128, 1024], F32))
        pv0 = en(nc.psum_tensor("pv0", [128, 512], F32))
        pv1 = en(nc.psum_tensor("pv1", [128, 512], F32))
        pb = [sAp[:, 0:512], sAp[:, 512:1024], sBp[:, 0:512], sBp[:, 512:1024],
              sCp[:, 0:512], sCp[:, 512:1024], pv0[:, :], pv1[:, :]]
        bankpair = [sAp[:, :], sBp[:, :], sCp[:, :]]

        sem_names = [
            "semT", "semV", "semA",
            "cqkv", "cwg", "cwin", "cwout", "ldc",
            "xldA", "qkst0", "qkst1", "vst0", "vst1",
            "kld", "vld", "qld0", "qld1",
            "xldB0", "xldB1", "xpst0", "xpst1", "xnst", "cbst",
            "xcld0", "xcld1", "xpldC0", "xpldC1",
            "outst0", "outst1", "gld", "xnts",
            "wgat", "xgat0", "xgat1", "scat", "strz", "strld0", "strld1",
            "ygat", "ostr0", "ostr1",
        ]
        S = {n: en(nc.semaphore(n)) for n in sem_names}

        identb = ar.alloc(BF16, [128])
        identf = ar.alloc(F32, [128])
        ones16 = ar.alloc(BF16, [128])
        onesf = ar.alloc(F32, [128])
        stat = ar.alloc(F32, [192])
        cbias = ar.alloc(F32, [128])
        bgt = ar.alloc(F32, [2])
        cvals = ar.alloc(F32, [4])
        rsq = ar.alloc(F32, [24])
        eiota = ar.alloc(F32, [8])
        tokid = ar.alloc(F32, [8])
        parange = ar.alloc(F32, [1])
        thr256 = ar.alloc(F32, [4])
        tilest = ar.alloc(F32, [16])

        p.emit("vector", lambda e: e.memset(cvals[:, 0:1], LN_EPS), inc=S["semV"])
        p.emit("vector", lambda e: e.memset(cvals[:, 1:2], -8.0), inc=S["semV"])
        p.emit("vector", lambda e: e.memset(cvals[:, 2:3], -0.5), inc=S["semV"])
        CV = p.val(S["semV"])

        for i, (dst, src) in enumerate(((identb, identb_h), (identf, identf_h),
                                        (ones16, ones16_h), (onesf, onesf_h))):
            p.emit("sync", lambda e, d=dst, s=src: e.dma_start(out=d, in_=s),
                   waits=[(S["ldc"], 16 * i)], inc=S["ldc"], incby=16)
        p.emit("sync", lambda e: e.dma_start(
            out=bgt, in_=bg1_h.rearrange("(c p) -> p c", p=128),
            allow_slow_non_contiguous=True),
            waits=[(S["ldc"], 64)], inc=S["ldc"], incby=16)
        for dst, srch in ((eiota, eiota_h), (tokid, tokid_h), (parange, parange_h),
                          (thr256, thr256_h), (tilest, tilest_h)):
            prev = p.val(S["ldc"])
            p.emit("sync", lambda e, d=dst, s=srch: e.dma_start(out=d, in_=s),
                   waits=[(S["ldc"], prev)], inc=S["ldc"], incby=16)
        LDC = dict(identb=16, identf=32, ones16=48, onesf=64, bg1=80,
                   eiota=96, tokid=112, parange=128, thr256=144, tilest=160,
                   masks=176)

        ar.mark()
        wqkv_sb = ar.alloc(BF16, [8, 3 * D])
        xA2 = ar.alloc(F32, [2, 4, D])
        xn1_tm2 = ar.alloc(BF16, [2, 4, D])
        xn1_fm2 = ar.alloc(BF16, [2, 8, QW])
        bnb2 = ar.alloc(F32, [2, 4, 12])
        mv2 = ar.alloc(F32, [2, 4, 2])
        rstdA2 = ar.alloc(F32, [2, 4])
        nmrsA2 = ar.alloc(F32, [2, 4])
        qkst = ar.alloc(BF16, [2, QW])
        sqb = ar.alloc(BF16, [2, QW])
        vstg = ar.alloc(BF16, [2, NH, 65])

        p.emit("gpsimd", lambda e: e.dma_start(
            out=wqkv_sb, in_=wqkv_h.rearrange("(k p) m -> p k m", p=128)),
            inc=S["cqkv"], incby=16)
        p.emit("gpsimd", lambda e: e.dma_start(
            out=wg1_scr.rearrange("p (k m) -> p k m", k=8, m=GDIM),
            in_=wg1_h.rearrange("(k p) m -> p k m", p=128)),
            waits=[(S["cqkv"], 16)], inc=S["cwg"], incby=16)
        p.emit("gpsimd", lambda e: e.dma_start(
            out=wg2_scr.rearrange("p (k m) -> p k m", k=2, m=E),
            in_=wg2_h.rearrange("(k p) m -> p k m", p=128)),
            waits=[(S["cwg"], 16)], inc=S["cwg"], incby=16)
        for ei in range(E):
            p.emit("gpsimd", lambda e, ei=ei: e.dma_start(
                out=wi_scr[ei].rearrange("p (k h) -> p k h", k=8, h=HH),
                in_=win_h[ei].rearrange("(k p) h -> p k h", p=128)),
                waits=[(S["cwg"], 32), (S["cwin"], 16 * ei)], inc=S["cwin"], incby=16)
        for ei in range(E):
            p.emit("gpsimd", lambda e, ei=ei: e.dma_start(
                out=wo_scr[ei].rearrange("p (k d) -> p k d", k=16, d=D),
                in_=wout_h[ei].rearrange("(k p) d -> p k d", p=128)),
                waits=[(S["cwin"], 16 * E), (S["cwout"], 16 * ei)],
                inc=S["cwout"], incby=16)

        p.emit("vector", lambda e: e.memset(vstg[:, :, :, 64:65], 1.0), inc=S["semV"])

        qk_stg = [None, None]
        qk_psg = [0, 0]
        v_stg = [0, 0]
        v_psg = [0, 0]
        evacA = [0, 0, 0, 0]

        xaA_war = [0, 0]
        xaV_war = [0, 0]
        for g in range(NG):
            sg = g % 2
            xA = xA2[:, sg]
            xn1_tm = xn1_tm2[:, sg]
            xn1_fm = xn1_fm2[:, sg]
            bnb = bnb2[:, sg]
            mv = mv2[:, sg]
            rstdA = rstdA2[:, sg]
            nmrsA = nmrsA2[:, sg]
            p.emit("sync", lambda e, g=g, xA=xA: e.dma_start(
                out=xA, in_=x_h[g * QW:(g + 1) * QW, :].rearrange(
                    "(j p) d -> p j d", p=128)),
                waits=[(S["semA"], xaA_war[sg]), (S["semV"], xaV_war[sg])],
                inc=S["xldA"], incby=16)
            xld = p.val(S["xldA"])

            for j in range(4):
                p.emit("vector", lambda e, j=j, bnb=bnb, xA=xA: e.bn_stats(
                    bnb[:, j, 0:6], xA[:, j, 0:512]),
                    waits=[(S["xldA"], xld)])
                p.emit("vector", lambda e, j=j, bnb=bnb, xA=xA: e.bn_stats(
                    bnb[:, j, 6:12], xA[:, j, 512:1024]),
                    inc=S["semV"] if j == 3 else None)
            vBn = p.val(S["semV"])
            for j in range(4):
                p.emit("vector", lambda e, j=j, mv=mv, bnb=bnb: e.bn_aggr(
                    mv[:, j, :], bnb[:, j, :]),
                    waits=[(S["semV"], vBn)], inc=S["semV"] if j == 3 else None)
            vStats = p.val(S["semV"])
            p.emit("scalar", lambda e, rstdA=rstdA, mv=mv: e.activation(
                out=rstdA, in_=mv[:, :, 1],
                func=AF.Ln, bias=cvals[:, 0:1], scale=1.0),
                waits=[(S["semV"], vStats), (S["semV"], CV)], inc=S["semA"])
            aLn = p.val(S["semA"])
            p.emit("scalar", lambda e, rstdA=rstdA: e.activation(
                out=rstdA, in_=rstdA, func=AF.Exp, bias=0.0, scale=-0.5),
                waits=[(S["semA"], aLn)], inc=S["semA"])
            aRstd = p.val(S["semA"])
            p.emit("vector", lambda e, nmrsA=nmrsA, mv=mv, rstdA=rstdA:
                e.scalar_tensor_tensor(
                    out=nmrsA, in0=mv[:, :, 0], scalar=-1.0, in1=rstdA,
                    op0=ALU.mult, op1=ALU.mult),
                waits=[(S["semA"], aRstd)], inc=S["semV"])
            vNm = p.val(S["semV"])

            for j in range(4):
                p.emit("scalar", lambda e, j=j, xn1_tm=xn1_tm, xA=xA,
                       nmrsA=nmrsA, rstdA=rstdA: e.activation(
                    out=xn1_tm[:, j, :], in_=xA[:, j, :], func=AF.Identity,
                    bias=nmrsA[:, j:j + 1], scale=rstdA[:, j:j + 1]),
                    waits=[(S["semV"], vNm)], inc=S["semA"])
                aApp = p.val(S["semA"])
                bank = pb[6 + (j % 2)]
                bview = bf16_bank(bank).rearrange("p (f c) -> p f c", f=8, c=128)
                for f in range(8):
                    p.emit("tensor", lambda e, j=j, f=f, bview=bview,
                           xn1_tm=xn1_tm: e.transpose(
                        bview[:, f, :], xn1_tm[:, j, 128 * f:128 * (f + 1)], identb),
                        waits=[(S["semA"], aApp), (S["semA"], evacA[j - 2] if j >= 2 else 0),
                               (S["ldc"], LDC["identb"])],
                        inc=S["semT"] if f == 7 else None)
                tT = p.val(S["semT"])
                p.emit("scalar", lambda e, j=j, bview=bview, xn1_fm=xn1_fm: e.copy(
                    out=xn1_fm[:, :, 128 * j:128 * (j + 1)], in_=bview[:, :, :]),
                    waits=[(S["semT"], tT)], inc=S["semA"])
                evacA[j] = p.val(S["semA"])
            aFm = evacA[3]
            xaA_war[sg] = p.val(S["semA"])
            xaV_war[sg] = vStats

            pend_stats = []
            sq_war = [0, 0]
            stat_war = [0]

            def emit_stats(m, slot, vSq, g=g):
                is_q = m < 8
                for par in range(2):
                    sb = pb[4 + par]
                    p.emit("tensor", lambda e, par=par, sb=sb, slot=slot: e.matmul(
                        sb[0:1, :], ones16[64 * par:64 * par + 64, 0:1],
                        sqb[64 * par:64 * par + 64, slot, :], start=True, stop=True),
                        waits=[(S["semV"], vSq), (S["semV"], stat_war[0]),
                               (S["ldc"], LDC["ones16"])],
                        inc=S["semT"])
                tStat = p.val(S["semT"])
                for par in range(2):
                    sb = pb[4 + par]
                    h = 2 * (m % 8) + par
                    if is_q:
                        dst = stat[0:1, h * 8 + g:h * 8 + g + 1]
                        p.emit("vector", lambda e, sb=sb, dst=dst: e.reduce_max(
                            dst, sb[0:1, :], axis=AX.X),
                            waits=[(S["semT"], tStat)], inc=S["semV"])
                    elif g == 0:
                        dst = stat[0:1, 128 + h:128 + h + 1]
                        p.emit("vector", lambda e, sb=sb, dst=dst: e.reduce_max(
                            dst, sb[0:1, :], axis=AX.X),
                            waits=[(S["semT"], tStat)], inc=S["semV"])
                    else:
                        tmp = stat[0:1, 144 + par:145 + par]
                        dst = stat[0:1, 128 + h:128 + h + 1]
                        p.emit("vector", lambda e, sb=sb, tmp=tmp: e.reduce_max(
                            tmp, sb[0:1, :], axis=AX.X),
                            waits=[(S["semT"], tStat)], inc=S["semV"])
                        vR = p.val(S["semV"])
                        p.emit("vector", lambda e, tmp=tmp, dst=dst: e.tensor_tensor(
                            out=dst, in0=dst, in1=tmp, op=ALU.max),
                            waits=[(S["semV"], vR)], inc=S["semV"])
                sq_war[slot] = p.val(S["semV"])
                stat_war[0] = p.val(S["semV"])

            for m in range(16):
                is_q = m < 8
                slot = m % 2
                bank = pb[slot]
                for k in range(8):
                    p.emit("tensor", lambda e, m=m, k=k, bank=bank,
                           xn1_fm=xn1_fm: e.matmul(
                        bank[:, :], wqkv_sb[:, k, 128 * m:128 * (m + 1)],
                        xn1_fm[:, k, :], start=(k == 0), stop=(k == 7)),
                        waits=[(S["semA"], aFm), (S["cqkv"], 16),
                               (S["semA"], qk_psg[slot])],
                        inc=S["semT"] if k == 7 else None)
                tQK = p.val(S["semT"])
                ew = [(S["semT"], tQK)]
                if qk_stg[slot] is not None:
                    ew.append(qk_stg[slot])
                p.emit("scalar", lambda e, slot=slot, bank=bank: e.copy(
                    out=qkst[:, slot, :], in_=bank[:, :]),
                    waits=ew, inc=S["semA"])
                aEvac = p.val(S["semA"])
                qk_psg[slot] = aEvac
                scr = q_scr if is_q else k_scr
                ssem = S[f"qkst{slot}"]
                hp = m % 8
                p.emit("sync", lambda e, hp=hp, g=g, slot=slot, scr=scr: e.dma_start(
                    out=scr[hp][:, g * QW:(g + 1) * QW], in_=qkst[:, slot, :]),
                    waits=[(S["semA"], aEvac)], inc=ssem, incby=16)
                qk_stg[slot] = (ssem, p.val(ssem))
                p.emit("vector", lambda e, slot=slot: e.tensor_tensor(
                    out=sqb[:, slot, :], in0=qkst[:, slot, :], in1=qkst[:, slot, :],
                    op=ALU.mult),
                    waits=[(S["semA"], aEvac), (S["semV"], sq_war[slot])],
                    inc=S["semV"])
                vSq = p.val(S["semV"])
                pend_stats.append((m, slot, vSq))
                if len(pend_stats) >= 2 or m == 15:
                    todo = pend_stats if m == 15 else pend_stats[:1]
                    for (pm, pslot, pvSq) in todo:
                        emit_stats(pm, pslot, pvSq)
                    del pend_stats[:len(todo)]

            for j in range(4):
                slot = j % 2
                bank = pb[2 + slot]
                for n in range(2):
                    for k in range(8):
                        p.emit("tensor", lambda e, j=j, n=n, k=k, bank=bank,
                               xn1_fm=xn1_fm: e.matmul(
                            bank[:, :], xn1_fm[:, k, 128 * j:128 * (j + 1)],
                            wqkv_sb[:, k, 2 * D + 512 * n: 2 * D + 512 * (n + 1)],
                            start=(k == 0), stop=(k == 7)),
                            waits=[(S["semA"], aFm), (S["semV"], v_psg[slot])],
                            inc=S["semT"] if k == 7 else None)
                    tV = p.val(S["semT"])
                    bview = bank[:, :].rearrange("p (a b) -> p a b", a=8, b=64)
                    p.emit("vector", lambda e, slot=slot, n=n, bview=bview: e.tensor_copy(
                        out=vstg[:, slot, 8 * n:8 * (n + 1), 0:64], in_=bview),
                        waits=[(S["semT"], tV), (S[f"vst{slot}"], v_stg[slot])],
                        inc=S["semV"])
                    v_psg[slot] = p.val(S["semV"])
                vEv = p.val(S["semV"])
                kc = 4 * g + j
                p.emit("sync", lambda e, kc=kc, slot=slot: e.dma_start(
                    out=v_scr[kc].rearrange("p (a b) -> p a b", a=NH, b=65),
                    in_=vstg[:, slot, :, :]),
                    waits=[(S["semV"], vEv)], inc=S[f"vst{slot}"], incby=16)
                v_stg[slot] = p.val(S[f"vst{slot}"])

        vAll = p.val(S["semV"])
        sq_q = stat[0:1, 0:128]
        sq_k = stat[0:1, 128:144]
        p.emit("scalar", lambda e: e.activation(out=sq_q, in_=sq_q, func=AF.Sqrt,
                                                bias=0.0, scale=1.0),
               waits=[(S["semV"], vAll)], inc=S["semA"])
        p.emit("scalar", lambda e: e.activation(out=sq_k, in_=sq_k, func=AF.Sqrt,
                                                bias=0.0, scale=1.0),
               waits=[(S["semA"], p.val(S["semA"]))], inc=S["semA"])
        aSq = p.val(S["semA"])
        for h in range(NH):
            p.emit("vector", lambda e, h=h: e.tensor_single_scalar(
                out=sq_q[0:1, 8 * h:8 * (h + 1)], in_=sq_q[0:1, 8 * h:8 * (h + 1)],
                scalar=sq_k[0:1, h:h + 1], op=ALU.mult),
                waits=[(S["semA"], aSq)], inc=S["semV"] if h == NH - 1 else None)
        vCm = p.val(S["semV"])
        p.emit("scalar", lambda e: e.activation(out=sq_q, in_=sq_q, func=AF.Identity,
                                                bias=cvals[0:1, 2:3], scale=-0.125 * 1.02),
               waits=[(S["semV"], vCm)], inc=S["semA"])
        aNegc = p.val(S["semA"])
        p.emit("tensor", lambda e: e.matmul(pb[6][:, 0:128], onesf[0:1, :], sq_q,
                                            start=True, stop=True),
               waits=[(S["semA"], aNegc), (S["ldc"], LDC["onesf"])], inc=S["semT"])
        tCb = p.val(S["semT"])
        p.emit("vector", lambda e: e.tensor_copy(out=cbias, in_=pb[6][:, 0:128]),
               waits=[(S["semT"], tCb)], inc=S["semV"])
        vCb = p.val(S["semV"])
        if debug:
            p.emit("sync", lambda e: e.dma_start(out=cb_scr, in_=cbias),
                   waits=[(S["semV"], vCb)], inc=S["cbst"], incby=16)

        A = dict(T=p.val(S["semT"]), V=p.val(S["semV"]), A=p.val(S["semA"]),
                 qk0=p.val(S["qkst0"]), qk1=p.val(S["qkst1"]),
                 v0=p.val(S["vst0"]), v1=p.val(S["vst1"]))
        ar.reset()

        if upto == "A":
            p.emit("sync", lambda e: e.nop(), waits=_dma_ends(p, S))
            _finalize(nc, p)
            return nc

        B = _phase_b(nc, p, S, ar, pb, bankpair, x_h, q_scr, k_scr, v_scr, masks_h,
                     xp_scr, xn2_scr, xn2tm_scr, identb, identf, cbias, cvals, rsq,
                     A, LDC)

        if upto == "B":
            p.emit("sync", lambda e: e.nop(), waits=_dma_ends(p, S))
            _finalize(nc, p)
            return nc

        C = _phase_c(nc, p, S, ar, pb, xp_scr, xn2_scr, xn2tm_scr, wg1_scr, wg2_scr,
                     wi_scr, wo_scr, stream_scr, gates_scr, o_scr, bgt, cvals, rsq,
                     identb, identf, onesf, eiota, tokid, parange, thr256, tilest,
                     out_h, B, LDC, dbgC)
        p.emit("sync", lambda e: e.nop(), waits=_dma_ends(p, S))
        _finalize(nc, p)
        return nc



def _dma_ends(p, S):
    ends = []
    for n, sem in S.items():
        if n in ("semT", "semV", "semA"):
            continue
        v = p.val(sem)
        if v > 0:
            ends.append((sem, v))
    return ends

def _finalize(nc, p):
    with nc.Block() as block:
        @block.tensor
        def _(e):
            for f in p.q["tensor"]:
                f(e)

        @block.vector
        def _(e):
            for f in p.q["vector"]:
                f(e)

        @block.scalar
        def _(e):
            for f in p.q["scalar"]:
                f(e)

        @block.gpsimd
        def _(e):
            for f in p.q["gpsimd"]:
                f(e)

        @block.sync
        def _(e):
            for f in p.q["sync"]:
                f(e)


def _phase_b(nc, p, S, ar, pb, bankpair, x_h, q_scr, k_scr, v_scr, masks_h,
             xp_scr, xn2_scr, xn2tm_scr, identb, identf, cbias, cvals, rsq, A, LDC):
    ar.mark()
    kbuf = ar.alloc(BF16, [HP, NT])
    vbuf = ar.alloc(BF16, [NKC, NH, 65])
    qbuf = ar.alloc(BF16, [2, HP, QW])
    masks = ar.alloc(BF16, [4, QW])
    pT = ar.alloc(BF16, [3, 2 * QW])
    pvb = ar.alloc(F32, [QW])
    attn_tm = ar.alloc(F32, [4, D])
    xB = ar.alloc(F32, [2, D])
    xpB = ar.alloc(F32, [2, D])
    xn2_tm = ar.alloc(BF16, [D])
    xn2_st = ar.alloc(BF16, [8, QW])
    rcol = ar.alloc(F32, [8])
    bnbB = ar.alloc(F32, [4, 12])
    mvB = ar.alloc(F32, [4, 2])
    rstdB = ar.alloc(F32, [8])
    nmrsB = ar.alloc(F32, [8])

    bwaits = [(S["semT"], A["T"]), (S["semV"], A["V"]), (S["semA"], A["A"]),
              (S["qkst0"], A["qk0"]), (S["qkst1"], A["qk1"]),
              (S["vst0"], A["v0"]), (S["vst1"], A["v1"])]
    p.emit("sync", lambda e: e.dma_start(
        out=kbuf, in_=k_scr.rearrange("hp p t -> p hp t")),
        waits=bwaits, inc=S["kld"], incby=16)
    p.emit("sync", lambda e: e.dma_start(
        out=vbuf, in_=v_scr.rearrange("kc p (h o) -> p kc h o", h=NH, o=65)),
        inc=S["vld"], incby=16)
    p.emit("sync", lambda e: e.dma_start(
        out=masks, in_=masks_h.rearrange("p (d w) -> p d w", d=4, w=QW)),
        waits=[(S["ldc"], 160)], inc=S["ldc"], incby=16)

    q_guard = [0, 0]
    x_guard = [0, 0]
    xp_guard = [0, 0]
    xn_guard = 0
    xnts_guard = [0]
    apply_guard = [0, 0]
    tr_guard = [0, 0]
    prev_epi_A = 0

    pending_epi = [None]
    for qt in range(NG):
        qsem = S["qld0"] if qt % 2 == 0 else S["qld1"]
        slotq = qt % 2
        p.emit("sync", lambda e, qt=qt, slotq=slotq: e.dma_start(
            out=qbuf[:, slotq, :, :],
            in_=q_scr[:, :, qt * QW:(qt + 1) * QW].rearrange("hp p t -> p hp t")),
            waits=[(S["semT"], q_guard[slotq])], inc=qsem, incby=16)
        qldv = p.val(qsem)
        nkc = 4 * qt + 4

        for h in range(NH):
            hp, par = h // 2, h % 2
            pvbank = pb[6 + h % 2]
            npair = nkc // 2
            exp_cnt = []
            pv_cnt = []
            sc_cnt = []

            def emit_scores(pr):
                g3 = pr % 3
                sp = bankpair[g3]
                for i in range(2):
                    kc = 2 * pr + i
                    w = [(qsem, qldv), (S["kld"], 16), (S["vld"], 16),
                         (S["semA"], prev_epi_A)]
                    if pr >= 3:
                        w.append((S["semA"], exp_cnt[pr - 3]))
                    p.emit("tensor", lambda e, hp=hp, par=par, kc=kc, i=i,
                           sp=sp, slotq=slotq: e.matmul(
                        sp[:, 512 * i:512 * (i + 1)],
                        kbuf[64 * par:64 * par + 64, hp, 128 * kc:128 * (kc + 1)],
                        qbuf[64 * par:64 * par + 64, slotq, hp, :],
                        start=True, stop=True),
                        waits=w, inc=S["semT"] if i == 1 else None)
                sc_cnt.append(p.val(S["semT"]))

            def emit_exp(pr):
                g3 = pr % 3
                sp = bankpair[g3]
                ew = [(S["semT"], sc_cnt[pr]), (S["semV"], A["V"])]
                if 2 * pr + 1 >= 4 * qt:
                    d = 2 * (pr - (npair - 2))
                    p.emit("vector", lambda e, sp=sp, d=d: e.tensor_tensor(
                        out=sp.rearrange("p (a b) -> p a b", a=2, b=512),
                        in0=sp.rearrange("p (a b) -> p a b", a=2, b=512),
                        in1=masks[:, d:d + 2, :], op=ALU.add),
                        waits=[(S["semT"], sc_cnt[pr]), (S["ldc"], LDC["masks"])],
                        inc=S["semV"])
                    ew = [(S["semV"], p.val(S["semV"])), (S["semV"], A["V"])]
                if pr >= 3:
                    ew.append((S["semT"], pv_cnt[pr - 3]))
                p.emit("scalar", lambda e, sp=sp, g3=g3, h=h, qt=qt: e.activation(
                    out=pT[:, g3, :], in_=sp, func=AF.Exp,
                    bias=cbias[:, h * 8 + qt:h * 8 + qt + 1], scale=0.125),
                    waits=ew, inc=S["semA"])
                exp_cnt.append(p.val(S["semA"]))

            def emit_pv(pr):
                g3 = pr % 3
                w = [(S["semA"], exp_cnt[pr])]
                if pr == 0:
                    w.append((S["semV"], tr_guard[h % 2]))
                for i in range(2):
                    kc = 2 * pr + i
                    p.emit("tensor", lambda e, kc=kc, h=h, g3=g3, i=i,
                           pvbank=pvbank, nkc=nkc: e.matmul(
                        pvbank[0:65, :], vbuf[:, kc, h, :],
                        pT[:, g3, 512 * i:512 * (i + 1)],
                        start=(kc == 0), stop=(kc == nkc - 1)),
                        waits=w if i == 0 else [(S["semA"], exp_cnt[pr])],
                        inc=S["semT"] if i == 1 else None)
                pv_cnt.append(p.val(S["semT"]))

            for pr in range(npair):
                emit_scores(pr)
                emit_exp(pr)
                if pr == 0 and pending_epi[0] is not None:
                    pending_epi[0]()
                    pending_epi[0] = None
                if pr >= 2:
                    emit_pv(pr - 2)
            emit_pv(npair - 2) if npair >= 2 else None
            emit_pv(npair - 1)

            def make_epi(h=h, pvbank=pvbank, tPV=None):
                tPV = pv_cnt[-1]

                def epi():
                    p.emit("scalar", lambda e, pvbank=pvbank: e.copy(
                        out=pvb[0:65, :], in_=pvbank[0:65, :]),
                        waits=[(S["semT"], tPV)], inc=S["semA"])
                    aPv = p.val(S["semA"])
                    trbank = pvbank
                    for j in range(4):
                        p.emit("tensor", lambda e, j=j, trbank=trbank: e.transpose(
                            trbank[:, 65 * j:65 * (j + 1)],
                            pvb[0:65, 128 * j:128 * (j + 1)],
                            identf[0:65, 0:65]),
                            waits=[(S["semA"], aPv), (S["ldc"], LDC["identf"])],
                            inc=S["semT"] if j == 3 else None)
                    tTr = p.val(S["semT"])
                    trv = trbank[:, 0:260].rearrange("p (a b) -> p a b", a=4, b=65)
                    p.emit("vector", lambda e, trv=trv: e.reciprocal(
                        rcol[:, 0:4], trv[:, :, 64]),
                        waits=[(S["semT"], tTr)], inc=S["semV"])
                    vRc = p.val(S["semV"])
                    for j in range(4):
                        p.emit("vector", lambda e, j=j, h=h, trv=trv: e.tensor_scalar(
                            out=attn_tm[:, j, 64 * h:64 * (h + 1)], in0=trv[:, j, 0:64],
                            scalar1=rcol[:, j:j + 1], scalar2=None, op0=ALU.mult),
                            waits=[(S["semV"], vRc)],
                            inc=S["semV"] if j == 3 else None)
                    tr_guard[h % 2] = p.val(S["semV"])
                return epi

            pending_epi[0] = make_epi()
        pending_epi[0]()
        pending_epi[0] = None
        q_guard[slotq] = p.val(S["semT"])
        vHeads = p.val(S["semV"])

        for j in range(4):
            p.emit("vector", lambda e, j=j: e.bn_stats(
                bnbB[:, j, 0:6], attn_tm[:, j, 0:512]),
                waits=[(S["semV"], vHeads), (S["semA"], prev_epi_A)])
            p.emit("vector", lambda e, j=j: e.bn_stats(
                bnbB[:, j, 6:12], attn_tm[:, j, 512:1024]),
                inc=S["semV"] if j == 3 else None)
        vBn2 = p.val(S["semV"])
        for j in range(4):
            p.emit("vector", lambda e, j=j: e.bn_aggr(mvB[:, j, :], bnbB[:, j, :]),
                   waits=[(S["semV"], vBn2)], inc=S["semV"] if j == 3 else None)
        vSt = p.val(S["semV"])
        p.emit("scalar", lambda e: e.activation(out=rstdB[:, 0:4], in_=mvB[:, :, 1],
                                                func=AF.Ln, bias=cvals[:, 0:1], scale=1.0),
               waits=[(S["semV"], vSt)], inc=S["semA"])
        p.emit("scalar", lambda e: e.activation(out=rstdB[:, 0:4], in_=rstdB[:, 0:4],
                                                func=AF.Exp, bias=0.0, scale=-0.5),
               waits=[(S["semA"], p.val(S["semA"]))], inc=S["semA"])
        aR = p.val(S["semA"])
        p.emit("vector", lambda e: e.scalar_tensor_tensor(
            out=nmrsB[:, 0:4], in0=mvB[:, :, 0], scalar=-1.0, in1=rstdB[:, 0:4],
            op0=ALU.mult, op1=ALU.mult),
            waits=[(S["semA"], aR)], inc=S["semV"])
        vNm = p.val(S["semV"])

        last_add = 0
        tXn_prev = 0
        for jp in range(2):
            for j in (2 * jp, 2 * jp + 1):
                slot = j % 2
                xsem = S["xldB0"] if slot == 0 else S["xldB1"]
                p.emit("sync", lambda e, qt=qt, j=j, slot=slot: e.dma_start(
                    out=xB[:, slot, :],
                    in_=x_h[qt * QW + 128 * j: qt * QW + 128 * (j + 1), :]),
                    waits=[(S["semV"], x_guard[slot])], inc=xsem, incby=16)
                xw = p.val(xsem)
                p.emit("scalar", lambda e, j=j: e.activation(
                    out=attn_tm[:, j, :], in_=attn_tm[:, j, :], func=AF.Identity,
                    bias=nmrsB[:, j:j + 1], scale=rstdB[:, j:j + 1]),
                    waits=[(S["semV"], vNm), (S["semV"], last_add)], inc=S["semA"])
                aL = p.val(S["semA"])
                p.emit("vector", lambda e, slot=slot, j=j: e.tensor_add(
                    out=xpB[:, slot, :], in0=xB[:, slot, :], in1=attn_tm[:, j, :]),
                    waits=[(S["semA"], aL), (xsem, xw),
                           (S[f"xpst{slot}"], xp_guard[slot]),
                           (S["semA"], apply_guard[slot])],
                    inc=S["semV"])
                last_add = p.val(S["semV"])
                x_guard[slot] = last_add
                vXp = last_add
                p.emit("sync", lambda e, qt=qt, j=j, slot=slot: e.dma_start(
                    out=xp_scr[qt * QW + 128 * j: qt * QW + 128 * (j + 1), :],
                    in_=xpB[:, slot, :]),
                    waits=[(S["semV"], vXp)], inc=S[f"xpst{slot}"], incby=16)
                xp_guard[slot] = p.val(S[f"xpst{slot}"])
                p.emit("vector", lambda e, slot=slot, j=j: e.bn_stats(
                    bnbB[:, j, 0:6], xpB[:, slot, 0:512]), waits=[(S["semV"], vXp)])
                p.emit("vector", lambda e, slot=slot, j=j: e.bn_stats(
                    bnbB[:, j, 6:12], xpB[:, slot, 512:1024]), inc=S["semV"])
                vBn3 = p.val(S["semV"])
                p.emit("vector", lambda e, j=j: e.bn_aggr(mvB[:, j, :], bnbB[:, j, :]),
                       waits=[(S["semV"], vBn3)], inc=S["semV"])
            vAggr2 = p.val(S["semV"])
            c0 = 4 + 2 * jp
            p.emit("scalar", lambda e, jp=jp, c0=c0: e.activation(
                out=rstdB[:, c0:c0 + 2], in_=mvB[:, 2 * jp:2 * jp + 2, 1],
                func=AF.Ln, bias=cvals[:, 0:1], scale=1.0),
                waits=[(S["semV"], vAggr2)], inc=S["semA"])
            p.emit("scalar", lambda e, c0=c0: e.activation(
                out=rstdB[:, c0:c0 + 2], in_=rstdB[:, c0:c0 + 2], func=AF.Exp,
                bias=0.0, scale=-0.5),
                waits=[(S["semA"], p.val(S["semA"]))], inc=S["semA"])
            aR2 = p.val(S["semA"])
            p.emit("vector", lambda e, jp=jp, c0=c0: e.scalar_tensor_tensor(
                out=nmrsB[:, c0:c0 + 2], in0=mvB[:, 2 * jp:2 * jp + 2, 0],
                scalar=-1.0, in1=rstdB[:, c0:c0 + 2], op0=ALU.mult, op1=ALU.mult),
                waits=[(S["semA"], aR2)], inc=S["semV"])
            vN2 = p.val(S["semV"])
            for j in (2 * jp, 2 * jp + 1):
                slot = j % 2
                p.emit("scalar", lambda e, slot=slot, j=j, jp=jp: e.activation(
                    out=xn2_tm, in_=xpB[:, slot, :], func=AF.Identity,
                    bias=nmrsB[:, 4 + j:5 + j], scale=rstdB[:, 4 + j:5 + j]),
                    waits=[(S["semV"], vN2), (S["xnts"], xnts_guard[0]),
                           (S["semT"], tXn_prev)], inc=S["semA"])
                aXn = p.val(S["semA"])
                apply_guard[slot] = aXn
                p.emit("sync", lambda e, qt=qt, j=j: e.dma_start(
                    out=xn2tm_scr[qt * QW + 128 * j: qt * QW + 128 * (j + 1), :],
                    in_=xn2_tm),
                    waits=[(S["semA"], aXn)], inc=S["xnts"], incby=16)
                xnts_guard[0] = p.val(S["xnts"])
                tb = pb[6 + (j % 2)]
                tbv = bf16_bank(tb).rearrange("p (f c) -> p f c", f=8, c=128)
                for f in range(8):
                    p.emit("tensor", lambda e, f=f, tbv=tbv: e.transpose(
                        tbv[:, f, :], xn2_tm[:, 128 * f:128 * (f + 1)], identb),
                        waits=[(S["semA"], aXn)],
                        inc=S["semT"] if f == 7 else None)
                tXn = p.val(S["semT"])
                tXn_prev = tXn
                p.emit("scalar", lambda e, j=j, tbv=tbv: e.copy(
                    out=xn2_st[:, :, 128 * j:128 * (j + 1)], in_=tbv[:, :, :]),
                    waits=[(S["semT"], tXn), (S["xnst"], xn_guard)], inc=S["semA"])
        aXnAll = p.val(S["semA"])
        p.emit("sync", lambda e, qt=qt: e.dma_start(
            out=xn2_scr[:, :, qt * QW:(qt + 1) * QW].rearrange("f p t -> p f t"),
            in_=xn2_st[:, :, :]),
            waits=[(S["semA"], aXnAll)], inc=S["xnst"], incby=16)
        xn_guard = p.val(S["xnst"])
        prev_epi_A = aXnAll

    res = dict(T=p.val(S["semT"]), V=p.val(S["semV"]), A=p.val(S["semA"]),
               xpst0=p.val(S["xpst0"]), xpst1=p.val(S["xpst1"]),
               xnst=p.val(S["xnst"]), xnts=p.val(S["xnts"]))
    ar.reset()
    return res


def _phase_c(nc, p, S, ar, pb, xp_scr, xn2_scr, xn2tm_scr, wg1_scr, wg2_scr,
             wi_scr, wo_scr, stream_scr, gates_scr, o_scr, bgt, cvals, rsq,
             identb, identf, onesf, eiota, tokid, parange, thr256, tilest,
             out_h, B, LDC, dbgC):
    import concourse.bass as bass_mod
    IOX = bass_mod.IndirectOffsetOnAxis
    ar.mark()
    TTW = 1024
    NTT = NT // TTW
    NTILE = 15
    wbuf = ar.alloc(BF16, [3, 8 * HH])
    xc = ar.alloc(BF16, [8, TTW])
    xg_tm = ar.alloc(BF16, [2, 2, D])
    xg_fm = ar.alloc(BF16, [8, 256])
    hsb = ar.alloc(BF16, [2, 16, 256])
    o_stage = ar.alloc(F32, [2, D])
    ybuf = ar.alloc(F32, [3, D])
    wg1_sb = ar.alloc(BF16, [8, GDIM])
    wg2_sb = ar.alloc(BF16, [2, E])
    gsb = ar.alloc(BF16, [2, TTW])
    gexp2 = ar.alloc(F32, [2, 64])
    dC = ar.alloc(F32, [8, 8])
    t1t = ar.alloc(F32, [8, 1])
    t2t = ar.alloc(F32, [8, 1])
    grs2 = ar.alloc(F32, [2, 16])
    rrC2 = ar.alloc(F32, [2, 8])
    bnbC = ar.alloc(F32, [12])
    mvC = ar.alloc(F32, [2])
    sclC = ar.alloc(F32, [4])
    xpC = ar.alloc(F32, [2, D])
    outst = ar.alloc(F32, [2, D])
    mask_fm = ar.alloc(F32, [TTW])
    pos_fm = ar.alloc(F32, [TTW])
    zeros_fm = ar.alloc(F32, [TTW])
    pos_tm = ar.alloc(F32, [8, 8])
    segrow = ar.alloc(F32, [4, 16])
    segb = ar.alloc(F32, [1])
    eA8 = ar.alloc(F32, [8, 1])
    mA3 = ar.alloc(F32, [8, 8])
    mB3 = ar.alloc(F32, [8, 8])
    rt3 = ar.alloc(F32, [8, 8])
    posf = ar.alloc(F32, [16])
    gf = ar.alloc(F32, [16])
    rdiv = ar.alloc(F32, [3, 16])
    fABi = ar.alloc(I32, [16])
    posABi2 = ar.alloc(I32, [2, 16])
    widxi2 = ar.alloc(I32, [2, 16])
    tokidi = ar.alloc(I32, [8])
    widxf = ar.alloc(F32, [16])
    dbg_sb = ar.alloc(F32, [128])
    nc._dbg_ap = dbg_sb
    strm = ar.alloc(I32, [2, 30])
    gstrm = ar.alloc(F32, [2, 30])
    zstr = ar.alloc(I32, [30])
    zstrf = ar.alloc(F32, [30])

    dC3 = dC

    bw = [(S["semT"], B["T"]), (S["semV"], B["V"]), (S["semA"], B["A"]),
          (S["xpst0"], B["xpst0"]), (S["xpst1"], B["xpst1"]),
          (S["xnst"], B["xnst"]), (S["xnts"], B["xnts"])]

    p.emit("sync", lambda e: e.dma_start(
        out=wg1_sb, in_=wg1_scr.rearrange("p (k m) -> p k m", k=8, m=GDIM)),
        waits=bw + [(S["cwg"], 32)], inc=S["gld"], incby=16)
    p.emit("sync", lambda e: e.dma_start(
        out=wg2_sb, in_=wg2_scr.rearrange("p (k m) -> p k m", k=2, m=E)),
        waits=[(S["gld"], 16)], inc=S["gld"], incby=16)
    p.emit("vector", lambda e: e.memset(zeros_fm[0:8, :], 0.0),
           waits=bw, inc=S["semV"])
    p.emit("vector", lambda e: e.memset(zstr[:, :], 0), inc=S["semV"])
    p.emit("vector", lambda e: e.memset(zstrf[:, :], 0.0), inc=S["semV"])
    p.emit("vector", lambda e: e.tensor_copy(out=tokidi, in_=tokid),
           waits=[(S["ldc"], LDC["tokid"])], inc=S["semV"])
    vZ = p.val(S["semV"])
    vTokidi = [p.val(S["semV"])]

    xc_guard = 0
    pg_guard = 0
    pl_guard = 0
    gs_guard = 0
    gexp_guard = 0
    route_guard = 0
    xpc_guard = [0, 0]
    out_guard = [0, 0]
    lnC_guard = 0
    wslot_guard = [0, 0, 0]
    xg_guard = [0, 0]
    ha_g = [0, 0]
    ob_g = [0, 0, 0, 0]
    hsb_guard = [0, 0]
    ostage_g = {0: [0, 0], 1: [0, 0]}
    wg_wi = {}
    wg_wo = {}
    xgat_hist = [0, 0]
    ygat_hist = [0, 0]
    strz_hist = [0, 0]
    nL = 0

    def emit_route(tt):
        nonlocal pg_guard, gs_guard, pl_guard, route_guard, xc_guard
        par = tt % 2
        RS = {}
        xgat = S["xgat0"] if par == 0 else S["xgat1"]
        strld = S["strld0"] if par == 0 else S["strld1"]
        ostr = S["ostr0"] if par == 0 else S["ostr1"]

        p.emit("sync", lambda e, par=par: e.dma_start(
            out=stream_scr[par].rearrange("(p c) -> p c", p=128), in_=zstr),
            waits=bw + [(S["semV"], vZ), (S["ygat"], ygat_hist[par])],
            inc=S["strz"], incby=16)
        p.emit("sync", lambda e, par=par: e.dma_start(
            out=gates_scr[par].rearrange("(p c) -> p c", p=128), in_=zstrf),
            inc=S["strz"], incby=16)
        strz_v = p.val(S["strz"])

        p.emit("sync", lambda e, tt=tt: e.dma_start(
            out=xc, in_=xn2_scr[:, :, tt * TTW:(tt + 1) * TTW].rearrange(
                "f p t -> p f t")),
            waits=[(S["semT"], xc_guard)],
            inc=S["xcld0" if par == 0 else "xcld1"], incby=16)
        xcv = p.val(S["xcld0" if par == 0 else "xcld1"])
        xsem = S["xcld0" if par == 0 else "xcld1"]

        for m2 in range(2):
            for th2 in range(2):
                for k in range(8):
                    p.emit("tensor", lambda e, m2=m2, th2=th2, k=k: e.matmul(
                        pb[6][:, :], wg1_sb[:, k, 128 * m2:128 * (m2 + 1)],
                        xc[:, k, 512 * th2:512 * (th2 + 1)],
                        start=(k == 0), stop=(k == 7)),
                        waits=[(xsem, xcv), (S["gld"], 32), (S["semA"], pg_guard),
                               (S["semV"], route_guard)],
                        inc=S["semT"] if k == 7 else None)
                tG = p.val(S["semT"])
                p.emit("scalar", lambda e, m2=m2, th2=th2: e.activation(
                    out=gsb[:, m2, 512 * th2:512 * (th2 + 1)], in_=pb[6][:, :],
                    func=AF.Gelu, bias=bgt[:, m2:m2 + 1], scale=1.0),
                    waits=[(S["semT"], tG), (S["ldc"], LDC["bg1"]),
                           (S["semT"], gs_guard)],
                    inc=S["semA"])
                pg_guard = p.val(S["semA"])
        aG = p.val(S["semA"])
        for tokM in range(8):
            for k2 in range(2):
                p.emit("tensor", lambda e, tokM=tokM, k2=k2: e.matmul(
                    pb[7][:, 8 * tokM:8 * (tokM + 1)],
                    gsb[:, k2, 128 * tokM:128 * (tokM + 1)], wg2_sb[:, k2, :],
                    start=(k2 == 0), stop=(k2 == 1)),
                    waits=[(S["semA"], aG), (S["semA"], pl_guard),
                           (S["semV"], route_guard)],
                    inc=S["semT"] if (tokM == 7 and k2 == 1) else None)
        tLog = p.val(S["semT"])
        gs_guard = tLog
        xc_guard = tLog
        p.emit("scalar", lambda e: e.activation(
            out=gexp2[:, par, :], in_=pb[7][:, 0:64], func=AF.Exp, bias=cvals[:, 1:2], scale=1.0),
            waits=[(S["semT"], tLog), (S["semV"], gexp_guard),
                   (S["ygat"], ygat_hist[par])], inc=S["semA"])
        pl_guard = p.val(S["semA"])
        aExp = pl_guard
        p.emit("vector", lambda e: e.tensor_reduce(
            grs2[:, par, 0:8], gexp2[:, par, :].rearrange("p (a b) -> p a b", a=8, b=8), axis=AX.X, op=ALU.add),
            waits=[(S["semA"], aExp)], inc=S["semV"])
        vS = p.val(S["semV"])
        p.emit("vector", lambda e: e.reciprocal(grs2[:, par, 8:16], grs2[:, par, 0:8]),
               waits=[(S["semV"], vS)], inc=S["semV"])
        p.emit("vector", lambda e: e.tensor_reduce(
            t1t[:, :, 0], gexp2[:, par, :].rearrange("p (a b) -> p a b", a=8, b=8), axis=AX.X, op=ALU.max), inc=S["semV"])
        vT1 = p.val(S["semV"])
        p.emit("vector", lambda e: e.tensor_tensor(
            out=dC3, in0=gexp2[:, par, :].rearrange("p (a b) -> p a b", a=8, b=8), in1=t1t.broadcast_to([128, 8, 8]), op=ALU.is_ge),
            waits=[(S["semV"], vT1)], inc=S["semV"])
        vGe = p.val(S["semV"])
        p.emit("vector", lambda e: e.scalar_tensor_tensor(
            out=dC3, in0=dC3, scalar=-1e6, in1=gexp2[:, par, :].rearrange("p (a b) -> p a b", a=8, b=8), op0=ALU.mult, op1=ALU.add),
            waits=[(S["semV"], vGe)], inc=S["semV"])
        vD2 = p.val(S["semV"])
        p.emit("vector", lambda e: e.tensor_reduce(
            t2t[:, :, 0], dC3, axis=AX.X, op=ALU.max),
            waits=[(S["semV"], vD2)], inc=S["semV"])
        vT2 = p.val(S["semV"])
        p.emit("vector", lambda e: e.tensor_tensor(
            out=dC3, in0=gexp2[:, par, :].rearrange("p (a b) -> p a b", a=8, b=8), in1=t2t.broadcast_to([128, 8, 8]), op=ALU.is_ge),
            waits=[(S["semV"], vT2)], inc=S["semV"])
        vM2 = p.val(S["semV"])
        p.emit("vector", lambda e, par=par: e.tensor_mul(
            out=gexp2[:, par, :], in0=gexp2[:, par, :], in1=dC),
               waits=[(S["semV"], vM2)], inc=S["semV"])
        vGates = p.val(S["semV"])
        p.emit("vector", lambda e: e.tensor_tensor(
            out=rrC2[:, par, :], in0=grs2[:, par, 8:16], in1=grs2[:, par, 8:16], op=ALU.mult),
            inc=S["semV"])
        vRr = p.val(S["semV"])

        for tokM in range(8):
            bank = pb[6] if tokM < 4 else pb[7]
            p.emit("tensor", lambda e, tokM=tokM, bank=bank: e.transpose(
                bank[0:8, 128 * (tokM % 4):128 * (tokM % 4 + 1)],
                dC[:, tokM, :], identf[:, :]),
                waits=[(S["semV"], vGates), (S["semA"], aExp),
                       (S["ldc"], LDC["identf"])],
                inc=S["semT"] if tokM % 4 == 3 else None)
            if tokM == 3:
                t03 = p.val(S["semT"])
            if tokM == 7:
                t47 = p.val(S["semT"])
        p.emit("vector", lambda e: e.tensor_copy(
            out=mask_fm[0:8, 0:512], in_=pb[6][0:8, :]),
            waits=[(S["semT"], t03)], inc=S["semV"])
        p.emit("vector", lambda e: e.tensor_copy(
            out=mask_fm[0:8, 512:1024], in_=pb[7][0:8, :]),
            waits=[(S["semT"], t47)], inc=S["semV"])
        vMfm = p.val(S["semV"])
        p.emit("vector", lambda e: e.tensor_reduce(
            segrow[0:8, 0, 0:1],
            mask_fm[0:8, :], axis=AX.X, op=ALU.add),
            waits=[(S["semV"], vMfm)], inc=S["semV"])
        vCnt = p.val(S["semV"])
        p.emit("tensor", lambda e: e.transpose(
            pb[6][0:1, 0:8], segrow[0:8, 0, 0:1], identf[0:8, 0:8]),
            waits=[(S["semV"], vCnt)], inc=S["semT"])
        tCnt = p.val(S["semT"])
        p.emit("vector", lambda e: e.tensor_copy(
            out=segrow[0:1, 1, 0:8], in_=pb[6][0:1, 0:8]),
            waits=[(S["semT"], tCnt)], inc=S["semV"])
        vCrow = p.val(S["semV"])
        p.emit("vector", lambda e: e.tensor_tensor(
            out=segrow[0:1, 2, 0:8].rearrange("p a -> p a 1").broadcast_to([1, 8, 4])
            if False else segrow[0:1, 3, 0:8],
            in0=segrow[0:1, 1, 0:8], in1=segrow[0:1, 1, 0:8], op=ALU.bypass),
            waits=[(S["semV"], vCrow)], inc=S["semV"]) if False else None
        p.emit("vector", lambda e: e.tensor_single_scalar(
            out=segrow[0:1, 2, 0:8], in_=segrow[0:1, 1, 0:8], scalar=0.5, op=ALU.is_gt),
            waits=[(S["semV"], vCrow)], inc=S["semV"])
        for jthr in (256.0, 512.0, 768.0):
            vv = p.val(S["semV"])
            p.emit("vector", lambda e, jthr=jthr: e.scalar_tensor_tensor(
                out=segrow[0:1, 2, 0:8], in0=segrow[0:1, 1, 0:8], scalar=jthr,
                in1=segrow[0:1, 2, 0:8], op0=ALU.is_gt, op1=ALU.add),
                waits=[(S["semV"], vv)], inc=S["semV"])
        vCeil = p.val(S["semV"])
        p.emit("vector", lambda e: e.tensor_single_scalar(
            out=segrow[0:1, 2, 0:8], in_=segrow[0:1, 2, 0:8], scalar=256.0,
            op=ALU.mult),
            waits=[(S["semV"], vCeil)], inc=S["semV"])
        vSz = p.val(S["semV"])
        p.emit("vector", lambda e: e.tensor_tensor_scan(
            out=segrow[0:1, 3, 0:8], data0=segrow[0:1, 2, 0:8],
            data1=zeros_fm[0:1, 0:8], initial=0.0, op0=ALU.add, op1=ALU.add),
            waits=[(S["semV"], vSz)], inc=S["semV"])
        vEnd = p.val(S["semV"])
        p.emit("vector", lambda e: e.tensor_sub(
            out=segrow[0:1, 0, 0:8], in0=segrow[0:1, 3, 0:8],
            in1=segrow[0:1, 2, 0:8]),
            waits=[(S["semV"], vEnd)], inc=S["semV"])
        vBase = p.val(S["semV"])
        p.emit("tensor", lambda e: e.transpose(
            pb[6][0:8, 8:9], segrow[0:1, 0, 0:8], identf[0:1, 0:1]),
            waits=[(S["semV"], vBase)], inc=S["semT"])
        tBase = p.val(S["semT"])
        p.emit("vector", lambda e: e.tensor_copy(out=segb[0:8, :], in_=pb[6][0:8, 8:9]),
               waits=[(S["semT"], tBase)], inc=S["semV"])
        vSegb = p.val(S["semV"])
        p.emit("vector", lambda e: e.tensor_tensor(
            out=widxf[0:1, 0:15].rearrange("p a -> p 1 a").broadcast_to([1, 8, 15])
            if False else widxf[0:1, 0:15],
            in0=widxf[0:1, 0:15], in1=widxf[0:1, 0:15], op=ALU.bypass),
            inc=None) if False else None
        p.emit("vector", lambda e: e.memset(widxf[0:1, 0:15], 0.0),
               waits=[(S["semV"], p.val(S["semV"]))], inc=S["semV"])
        for ee in range(8):
            vv = p.val(S["semV"])
            p.emit("vector", lambda e, ee=ee: e.scalar_tensor_tensor(
                out=widxf[0:1, 0:15], in0=tilest[0:1, 0:15],
                scalar=segrow[0:1, 3, ee:ee + 1], in1=widxf[0:1, 0:15],
                op0=ALU.is_ge, op1=ALU.add),
                waits=[(vv and S["semV"] or S["semV"], vv), (S["semV"], vEnd),
                       (S["ldc"], LDC["tilest"])], inc=S["semV"])
        vTe = p.val(S["semV"])
        p.emit("vector", lambda e: e.tensor_single_scalar(
            out=widxf[0:1, 0:15], in_=widxf[0:1, 0:15], scalar=7.0, op=ALU.min),
            waits=[(S["semV"], vTe)], inc=S["semV"])
        vTe2 = p.val(S["semV"])
        p.emit("tensor", lambda e: e.matmul(
            pb[7][:, 0:15], onesf[0:1, :], widxf[0:1, 0:15], start=True, stop=True),
            waits=[(S["semV"], vTe2), (S["ldc"], LDC["onesf"])], inc=S["semT"])
        tWx = p.val(S["semT"])
        p.emit("vector", lambda e: e.scalar_tensor_tensor(
            out=widxf[:, 0:15], in0=pb[7][:, 0:15], scalar=128.0,
            in1=parange.broadcast_to([128, 15]), op0=ALU.mult, op1=ALU.add),
            waits=[(S["semT"], tWx), (S["ldc"], LDC["parange"])], inc=S["semV"])
        vWx = p.val(S["semV"])
        p.emit("vector", lambda e, par=par: e.tensor_copy(
            out=widxi2[:, par, 0:15], in_=widxf[:, 0:15]),
               waits=[(S["semV"], vWx), (S["wgat"], p.val(S["wgat"]))],
               inc=S["semV"])
        vWxi = p.val(S["semV"])
        p.emit("vector", lambda e: e.tensor_tensor_scan(
            out=pos_fm[0:8, :], data0=mask_fm[0:8, :], data1=zeros_fm[0:8, :],
            initial=0.0, op0=ALU.add, op1=ALU.add),
            waits=[(S["semV"], vMfm)], inc=S["semV"])
        vScan = p.val(S["semV"])
        p.emit("vector", lambda e: e.tensor_scalar(
            out=pos_fm[0:8, :], in0=pos_fm[0:8, :], scalar1=segb[0:8, 0:1],
            scalar2=-1.0, op0=ALU.add, op1=ALU.add),
            waits=[(S["semV"], vScan), (S["semV"], vSegb)], inc=S["semV"])
        vPos = p.val(S["semV"])
        for tokM in range(8):
            bank = pb[6]
            p.emit("tensor", lambda e, tokM=tokM: e.transpose(
                pb[6][:, 8 * tokM:8 * (tokM + 1)],
                pos_fm[0:8, 128 * tokM:128 * (tokM + 1)], identf[0:8, 0:8]),
                waits=[(S["semV"], vPos), (S["semV"], vSegb)],
                inc=S["semT"] if tokM == 7 else None)
        tPtm = p.val(S["semT"])
        p.emit("vector", lambda e: e.tensor_copy(
            out=pos_tm[:, :, :], in_=pb[6][:, 0:64].rearrange(
                "p (a b) -> p a b", a=8, b=8)),
            waits=[(S["semT"], tPtm)], inc=S["semV"])
        vPtm = p.val(S["semV"])

        eiota3 = eiota.unsqueeze(1).broadcast_to([128, 8, 8])
        scat_prev = p.val(S["scat"])
        ygat_prev = p.val(S["ygat"])
        p.emit("vector", lambda e: e.scalar_tensor_tensor(
            out=rt3, in0=dC3, scalar=10.0, in1=eiota3, op0=ALU.mult,
            op1=ALU.subtract),
            waits=[(S["semV"], vPtm), (S["ldc"], LDC["eiota"]),
                   (S["scat"], scat_prev), (S["ygat"], ygat_prev)],
            inc=S["semV"])
        p.emit("vector", lambda e: e.tensor_reduce(
            eA8, rt3, axis=AX.X, op=ALU.max),
            waits=[(S["semV"], p.val(S["semV"]))], inc=S["semV"])
        p.emit("vector", lambda e: e.tensor_scalar(
            out=eA8, in0=eA8, scalar1=-1.0, scalar2=10.0, op0=ALU.mult, op1=ALU.add),
            waits=[(S["semV"], p.val(S["semV"]))], inc=S["semV"])
        p.emit("vector", lambda e: e.tensor_tensor(
            out=mA3, in0=eiota3, in1=eA8.broadcast_to([128, 8, 8]), op=ALU.is_equal),
            waits=[(S["semV"], p.val(S["semV"]))], inc=S["semV"])
        p.emit("vector", lambda e: e.tensor_sub(out=mB3, in0=dC3, in1=mA3),
               waits=[(S["semV"], p.val(S["semV"]))], inc=S["semV"])
        for (msk3, base) in ((mA3, 0), (mB3, 8)):
            p.emit("vector", lambda e, msk3=msk3: e.tensor_mul(
                out=rt3, in0=pos_tm, in1=msk3),
                waits=[(S["semV"], p.val(S["semV"]))], inc=S["semV"])
            p.emit("vector", lambda e, base=base: e.tensor_reduce(
                posf[:, base:base + 8], rt3, axis=AX.X, op=ALU.add),
                waits=[(S["semV"], p.val(S["semV"]))], inc=S["semV"])
            p.emit("vector", lambda e, msk3=msk3: e.tensor_mul(
                out=rt3, in0=gexp2[:, par, :].rearrange("p (a b) -> p a b", a=8, b=8), in1=msk3),
                waits=[(S["semV"], p.val(S["semV"]))], inc=S["semV"])
            p.emit("vector", lambda e, base=base: e.tensor_reduce(
                gf[:, base:base + 8], rt3, axis=AX.X, op=ALU.add),
                waits=[(S["semV"], p.val(S["semV"]))], inc=S["semV"])
        p.emit("vector", lambda e: e.tensor_copy(out=posABi2[:, par, :], in_=posf),
               waits=[(S["semV"], p.val(S["semV"]))], inc=S["semV"])
        p.emit("vector", lambda e: e.tensor_copy(out=rdiv[:, 2, :], in_=posf),
               inc=S["semV"])
        p.emit("vector", lambda e: e.memset(rdiv[:, 0, :], 0.0), inc=S["semV"])
        for sub in (2048.0, 1024.0, 512.0, 256.0, 128.0):
            p.emit("vector", lambda e, sub=sub: e.tensor_single_scalar(
                out=rdiv[:, 1, :], in_=rdiv[:, 2, :], scalar=sub - 0.5, op=ALU.is_gt),
                waits=[(S["semV"], p.val(S["semV"]))], inc=S["semV"])
            p.emit("vector", lambda e, sub=sub: e.scalar_tensor_tensor(
                out=rdiv[:, 2, :], in0=rdiv[:, 1, :], scalar=-sub, in1=rdiv[:, 2, :],
                op0=ALU.mult, op1=ALU.add),
                waits=[(S["semV"], p.val(S["semV"]))], inc=S["semV"])
            p.emit("vector", lambda e, sub=sub: e.scalar_tensor_tensor(
                out=rdiv[:, 0, :], in0=rdiv[:, 1, :], scalar=sub / 128.0,
                in1=rdiv[:, 0, :], op0=ALU.mult, op1=ALU.add),
                waits=[(S["semV"], p.val(S["semV"]))], inc=S["semV"])
        p.emit("vector", lambda e: e.scalar_tensor_tensor(
            out=rdiv[:, 1, :], in0=rdiv[:, 2, :], scalar=30.0, in1=rdiv[:, 0, :],
            op0=ALU.mult, op1=ALU.add),
            waits=[(S["semV"], p.val(S["semV"]))], inc=S["semV"])
        p.emit("vector", lambda e: e.tensor_copy(out=fABi, in_=rdiv[:, 1, :]),
               waits=[(S["semV"], p.val(S["semV"]))], inc=S["semV"])
        vPrep = p.val(S["semV"])
        for tokM in range(8):
            for (slot_i, gbase) in ((0, 0), (1, 8)):
                col = slot_i * 8 + tokM
                p.emit("gpsimd", lambda e, par=par, col=col, tokM=tokM:
                       e.indirect_dma_start(
                    out=stream_scr[par].unsqueeze(0),
                    out_offset=IOX(fABi[:, col:col + 1], 1),
                    in_=tokidi[:, tokM:tokM + 1], in_offset=None),
                    waits=[(S["semV"], vPrep), (S["strz"], strz_v),
                           (S["semV"], vTokidi[0])],
                    inc=S["scat"], incby=16)
                p.emit("gpsimd", lambda e, par=par, col=col: e.indirect_dma_start(
                    out=gates_scr[par].unsqueeze(0),
                    out_offset=IOX(fABi[:, col:col + 1], 1),
                    in_=gf[:, col:col + 1], in_offset=None),
                    inc=S["scat"], incby=16)
        scat_all = p.val(S["scat"])
        p.emit("sync", lambda e, par=par: e.dma_start(
            out=strm[:, par, :],
            in_=stream_scr[par].rearrange("(p c) -> p c", p=128)),
            waits=[(S["scat"], scat_all),
                   (S["xgat0"] if par == 0 else S["xgat1"], xgat_hist[par])],
            inc=strld, incby=16)
        p.emit("sync", lambda e, par=par: e.dma_start(
            out=gstrm[:, par, :],
            in_=gates_scr[par].rearrange("(p c) -> p c", p=128)),
            inc=strld, incby=16)
        strld_v = p.val(strld)

        route_guard = p.val(S["semV"])
        RS.update(dict(par=par, xgat=xgat, strld=strld, ostr=ostr,
                       strld_v=strld_v, vGates=vGates, vRr=vRr, vWxi=vWxi,
                       xcv=xcv, xsem=xsem))
        return RS

    route_states = {}
    pend_epi = None
    ostr_hist = {0: 0, 1: 0}
    yln_hist = {0: 0, 1: 0, 2: 0}

    def emit_epi_chunk(tt, par, ostr_all, g):
        nonlocal lnC_guard
        so = g % 2
        yso = g % 3
        xpsem = S["xpldC0"] if so == 0 else S["xpldC1"]
        otsem = S["outst0"] if so == 0 else S["outst1"]
        yg0 = p.val(S["ygat"])
        p.emit("gpsimd", lambda e, g=g, yso=yso, par=par: e.indirect_dma_start(
            out=ybuf[:, yso, :], out_offset=None,
            in_=o_scr[par], in_offset=IOX(posABi2[:, par, g:g + 1], 0)),
            waits=[(S["ostr0" if par == 0 else "ostr1"], ostr_all),
                   (S["ygat"], yg0), (S["semA"], yln_hist[yso])],
            inc=S["ygat"], incby=16)
        yg1 = p.val(S["ygat"])
        p.emit("gpsimd", lambda e, g=g, yso=yso, par=par: e.indirect_dma_start(
            out=ybuf[:, yso, :], out_offset=None,
            in_=o_scr[par], in_offset=IOX(posABi2[:, par, 8 + g:9 + g], 0),
            compute_op=ALU.add),
            waits=[(S["ygat"], yg1)],
            inc=S["ygat"], incby=16)
        ygv = p.val(S["ygat"])
        p.emit("sync", lambda e, tt=tt, g=g, so=so: e.dma_start(
            out=xpC[:, so, :],
            in_=xp_scr[tt * TTW + 128 * g: tt * TTW + 128 * (g + 1), :]),
            waits=[(S["semV"], xpc_guard[so])], inc=xpsem, incby=16)
        xpv = p.val(xpsem)
        p.emit("vector", lambda e, g=g, yso=yso: e.bn_stats(
            bnbC[:, 0:6], ybuf[:, yso, 0:512]),
            waits=[(S["ygat"], ygv)])
        p.emit("vector", lambda e, g=g, yso=yso: e.bn_stats(
            bnbC[:, 6:12], ybuf[:, yso, 512:1024]), inc=S["semV"])
        vbn = p.val(S["semV"])
        p.emit("vector", lambda e: e.bn_aggr(mvC, bnbC),
               waits=[(S["semV"], vbn)], inc=S["semV"])
        vmv = p.val(S["semV"])
        p.emit("vector", lambda e, g=g, par=par: e.tensor_tensor(
            out=sclC[:, 0:1], in0=mvC[:, 1:2], in1=rrC2[:, par, g:g + 1],
            op=ALU.mult),
            waits=[(S["semV"], vmv), (S["semV"], vRr)], inc=S["semV"])
        vvsc = p.val(S["semV"])
        p.emit("scalar", lambda e: e.activation(
            out=sclC[:, 1:2], in_=sclC[:, 0:1], func=AF.Ln,
            bias=cvals[:, 0:1], scale=1.0),
            waits=[(S["semV"], vvsc)], inc=S["semA"])
        p.emit("scalar", lambda e: e.activation(
            out=sclC[:, 1:2], in_=sclC[:, 1:2], func=AF.Exp, bias=0.0, scale=-0.5),
            waits=[(S["semA"], p.val(S["semA"]))], inc=S["semA"])
        aRstd3 = p.val(S["semA"])
        p.emit("vector", lambda e, g=g, par=par: e.tensor_tensor(
            out=sclC[:, 2:3], in0=sclC[:, 1:2], in1=grs2[:, par, 8 + g:9 + g],
            op=ALU.mult),
            waits=[(S["semA"], aRstd3)], inc=S["semV"])
        vS3 = p.val(S["semV"])
        p.emit("vector", lambda e: e.scalar_tensor_tensor(
            out=sclC[:, 3:4], in0=mvC[:, 0:1], scalar=-1.0, in1=sclC[:, 2:3],
            op0=ALU.mult, op1=ALU.mult),
            waits=[(S["semV"], vS3)], inc=S["semV"])
        vB3 = p.val(S["semV"])
        p.emit("scalar", lambda e, g=g, so=so, yso=yso: e.activation(
            out=outst[:, so, :], in_=ybuf[:, yso, :], func=AF.Identity,
            bias=sclC[:, 3:4], scale=sclC[:, 2:3]),
            waits=[(S["semV"], vB3), (otsem, out_guard[so])], inc=S["semA"])
        lnC_guard = p.val(S["semA"])
        yln_hist[yso] = lnC_guard
        aLn3 = lnC_guard
        p.emit("vector", lambda e, so=so: e.tensor_add(
            out=outst[:, so, :], in0=outst[:, so, :], in1=xpC[:, so, :]),
            waits=[(S["semA"], aLn3), (xpsem, xpv)], inc=S["semV"])
        vOut = p.val(S["semV"])
        xpc_guard[so] = vOut
        p.emit("sync", lambda e, tt=tt, g=g, so=so: e.dma_start(
            out=out_h[tt * TTW + 128 * g: tt * TTW + 128 * (g + 1), :],
            in_=outst[:, so, :]),
            waits=[(S["semV"], vOut)], inc=otsem, incby=16)
        out_guard[so] = p.val(otsem)
        if g == 7:
            ygat_hist[par] = p.val(S["ygat"])

    for tt in range(NTT):
        if tt == 0:
            route_states[0] = emit_route(0)
        RS = route_states[tt]
        par = RS["par"]
        xgat = RS["xgat"]
        strld = RS["strld"]
        ostr = RS["ostr"]
        strld_v = RS["strld_v"]
        vGates = RS["vGates"]
        vRr = RS["vRr"]
        vWxi = RS["vWxi"]
        def emit_wgather(j, tt=tt, parts=(0, 1)):
            nonlocal nL
            for (which, scr) in [(0, wi_scr), (1, wo_scr)][parts[0]:parts[-1] + 1]:
                slot = nL % 3
                g0 = p.val(S["wgat"])
                p.emit("gpsimd", lambda e, j=j, which=which, scr=scr, slot=slot,
                       par=par: e.indirect_dma_start(
                    out=wbuf[:, slot, :], out_offset=None,
                    in_=scr.rearrange("e p w -> (e p) w"),
                    in_offset=IOX(widxi2[:, par, j:j + 1], 0)),
                    waits=[(S["semV"], vWxi), (S["wgat"], g0),
                           (S["semT"], wslot_guard[slot]),
                           (S["cwin"], 16 * E), (S["cwout"], 16 * E)],
                    inc=S["wgat"], incby=16)
                nL += 1
                if which == 0:
                    wg_wi[(tt, j)] = (16 * nL, slot)
                else:
                    wg_wo[(tt, j)] = (16 * nL, slot)

        def emit_xgather(j, tt=tt, par=par, xgat=xgat, strld_v=strld_v):
            slot = j % 2
            x0 = p.val(xgat)
            for i in range(2):
                def _dbg(e, j=j, i=i, slot=slot, tt=tt, par=par):
                    try:
                        return e.indirect_dma_start(
                            out=xg_tm[:, slot, i, :], out_offset=None,
                            in_=xn2tm_scr,
                            in_offset=IOX(strm[:, par, :][:, 2 * j + i:2 * j + i + 1], 0),
                            element_offset=tt * TTW * D)
                    except Exception:
                        print("XGFAIL tt", tt, "j", j, "i", i, "slot", slot, "par", par)
                        print("  out:", xg_tm[:, slot, i, :])
                        print("  idx:", strm[:, par, :][:, 2 * j + i:2 * j + i + 1])
                        raise
                p.emit("gpsimd", lambda e, f=_dbg: f(e),
                    waits=[(strld, strld_v), (xgat, x0),
                           (S["semT"], xg_guard[slot]),
                           (S["xnts"], B["xnts"])],
                    inc=xgat, incby=16)
                x0 = p.val(xgat)


        emit_wgather(0)
        emit_xgather(0)
        h_end = [0] * NTILE
        o_end = [0] * NTILE
        for j in range(NTILE):
            if j == 10 and tt + 1 < NTT:
                route_states[tt + 1] = emit_route(tt + 1)
            slot = j % 2
            xgv = p.val(xgat)
            wgv = 16 * nL
            wg_wi_cnt, wslot_wi = wg_wi[(tt, j)]
            wg_wo_cnt, wslot_wo = wg_wo[(tt, j)]
            wi_v = wbuf[:, wslot_wi, :].rearrange("p (a b) -> p a b", a=8, b=HH)
            wo_v = wbuf[:, wslot_wo, :].rearrange("p (a b) -> p a b", a=16, b=D)
            if j + 1 < NTILE:
                emit_wgather(j + 1, parts=(0, 0))
                emit_xgather(j + 1)
            for i in range(2):
                bank = pb[i]
                bv = bf16_bank(bank).rearrange("p (f c) -> p f c", f=8, c=128)
                for f in range(8):
                    p.emit("tensor", lambda e, i=i, f=f, slot=slot, bv=bv: e.transpose(
                        bv[:, f, :], xg_tm[:, slot, i, 128 * f:128 * (f + 1)], identb),
                        waits=[(xgat, xgv), (S["semA"], ha_g[i]),
                               (S["ldc"], LDC["identb"])],
                        inc=S["semT"] if f == 7 else None)
                tT = p.val(S["semT"])
                p.emit("scalar", lambda e, i=i, bv=bv: e.copy(
                    out=xg_fm[:, :, :, 128 * i:128 * (i + 1)].squeeze(1)
                    if False else xg_fm[:, :, 128 * i:128 * (i + 1)],
                    in_=bv[:, :, :]),
                    waits=[(S["semT"], tT)], inc=S["semA"])
                ha_g[i] = p.val(S["semA"])
            aXgT = p.val(S["semA"])
            xg_guard[slot] = p.val(S["semT"])
            gelu_last = 0
            for m in range(16):
                bank = pb[m % 2]
                for k in range(8):
                    p.emit("tensor", lambda e, m=m, k=k, bank=bank, wi_v=wi_v:
                           e.matmul(
                        bank[:, 0:256], wi_v[:, k, 128 * m:128 * (m + 1)],
                        xg_fm[:, k, :], start=(k == 0), stop=(k == 7)),
                        waits=[(S["wgat"], wg_wi_cnt), (S["semA"], aXgT),
                               (S["semA"], ha_g[m % 2])],
                        inc=S["semT"] if k == 7 else None)
                tH = p.val(S["semT"])
                p.emit("scalar", lambda e, m=m, bank=bank, j=j: e.activation(
                    out=hsb[:, j % 2, m, :], in_=bank[:, 0:256], func=AF.Gelu,
                    bias=0.0, scale=1.0),
                    waits=[(S["semT"], tH), (S["semT"], hsb_guard[j % 2])],
                    inc=S["semA"])
                ha_g[m % 2] = p.val(S["semA"])
                gelu_last = ha_g[m % 2]
            h_end[j] = p.val(S["semT"])
            wslot_guard[wslot_wi] = h_end[j]
            if j + 1 < NTILE:
                emit_wgather(j + 1, parts=(1, 1))
            for i in range(2):
                for dn in range(2):
                    bi = 2 + 2 * i + dn
                    bank = pb[bi]
                    for k in range(16):
                        p.emit("tensor", lambda e, i=i, dn=dn, k=k, bank=bank,
                               wo_v=wo_v, j=j: e.matmul(
                            bank[:, :], hsb[:, j % 2, k, 128 * i:128 * (i + 1)],
                            wo_v[:, k, 512 * dn:512 * (dn + 1)],
                            start=(k == 0), stop=(k == 15)),
                            waits=[(S["semA"], gelu_last),
                                   (S["wgat"], wg_wo_cnt),
                                   (S["semV"], ob_g[bi - 2])],
                            inc=S["semT"] if k == 15 else None)
                    tO = p.val(S["semT"])
                    gcol = gstrm[:, par, 2 * j + i:2 * j + i + 1]
                    p.emit("vector", lambda e, bank=bank, i=i, dn=dn, gcol=gcol:
                           e.tensor_scalar(
                        out=o_stage[:, i, 512 * dn:512 * (dn + 1)], in0=bank[:, :],
                        scalar1=gcol, scalar2=None, op0=ALU.mult),
                        waits=[(S["semT"], tO), (strld, strld_v),
                               (ostr, ostage_g[par][i]),
                               (S["ostr1" if par == 0 else "ostr0"],
                                ostr_hist[1 - par])],
                        inc=S["semV"])
                    ob_g[bi - 2] = p.val(S["semV"])
                vEv = p.val(S["semV"])
                p.emit("sync", lambda e, j=j, i=i, par=par: e.dma_start(
                    out=o_scr[par][128 * (2 * j + i):128 * (2 * j + i + 1), :],
                    in_=o_stage[:, i, :]),
                    waits=[(S["semV"], vEv), (S["ygat"], ygat_hist[par]),
                           (ostr, p.val(ostr))],
                    inc=ostr, incby=16)
                ostage_g[par][i] = p.val(ostr)
            o_end[j] = p.val(S["semT"])
            hsb_guard[j % 2] = o_end[j]
            wslot_guard[wslot_wo] = o_end[j]
            if pend_epi is not None and 2 <= j:
                emit_epi_chunk(*pend_epi, g=j - 2)
                if j == 9:
                    pend_epi = None
        xgat_hist[par] = p.val(xgat)
        ostr_all = p.val(ostr)
        ostr_hist[par] = ostr_all
        pend_epi = (tt, par, ostr_all)
        gexp_guard = p.val(S["semV"])
        route_guard = p.val(S["semV"])

    if pend_epi is not None:
        for g in range(8):
            emit_epi_chunk(*pend_epi, g=g)
        pend_epi = None

    res = dict(outst=p.val(S["outst0"]) + p.val(S["outst1"]))
    ar.reset()
    return res


_consts = None


def _make_consts():
    global _consts
    if _consts is None:
        identf = np.eye(128, dtype=np.float32)
        masks = np.zeros((128, 4 * QW), dtype=np.float32)
        k = np.arange(128)[:, None]
        q = np.arange(QW)[None, :]
        for d in range(4):
            masks[:, d * QW:(d + 1) * QW] = np.where(q >= k + 128 * d, 0.0, -30000.0)
        pcol = np.arange(128, dtype=np.float32)[:, None]
        _consts = dict(
            identf=identf,
            identb=identf.astype(ml_dtypes.bfloat16),
            ones16=np.ones((128, 128), dtype=ml_dtypes.bfloat16),
            onesf=np.ones((128, 128), dtype=np.float32),
            masks=masks.astype(ml_dtypes.bfloat16),
            eiota=np.broadcast_to(np.arange(8, dtype=np.float32), (128, 8)).copy(),
            tokid=(pcol + 128.0 * np.arange(8, dtype=np.float32)[None, :]
                   ).astype(np.float32),
            parange=pcol.astype(np.float32).copy(),
            thr256=np.broadcast_to(np.array([0.0, 256.0, 512.0, 768.0], np.float32),
                                   (128, 4)).copy(),
            tilest=np.broadcast_to(256.0 * np.arange(16, dtype=np.float32),
                                   (128, 16)).copy(),
        )
    return _consts


def make_in_maps(L0_attn_x, L0_attn_W_qkv, L0_ffn_wg1, L0_ffn_bg1, L0_ffn_wg2,
                 L0_ffn_w_in, L0_ffn_w_out):
    x = np.ascontiguousarray(np.asarray(L0_attn_x, dtype=np.float32))
    base = dict(
        wqkv=np.ascontiguousarray(np.asarray(L0_attn_W_qkv, np.float32)),
        wg1=np.ascontiguousarray(np.asarray(L0_ffn_wg1, np.float32)),
        bg1=np.ascontiguousarray(np.asarray(L0_ffn_bg1, np.float32)),
        wg2=np.ascontiguousarray(np.asarray(L0_ffn_wg2, np.float32)),
        w_in=np.ascontiguousarray(np.asarray(L0_ffn_w_in, np.float32)),
        w_out=np.ascontiguousarray(np.asarray(L0_ffn_w_out, np.float32)),
        **_make_consts(),
    )
    return [dict(base, x=x[cc]) for cc in range(8)]


def kernel(L0_attn_x, L0_attn_W_qkv, L0_ffn_wg1, L0_ffn_bg1, L0_ffn_wg2,
           L0_ffn_w_in, L0_ffn_w_out):
    in_maps = make_in_maps(L0_attn_x, L0_attn_W_qkv, L0_ffn_wg1, L0_ffn_bg1,
                           L0_ffn_wg2, L0_ffn_w_in, L0_ffn_w_out)
    nc = build_program(upto="C", debug=False)
    res = run_bass_kernel_spmd(nc, in_maps, core_ids=list(range(8)))
    return np.stack([np.asarray(r["out"], np.float32) for r in res.results], axis=0)


# version 18
# speedup vs baseline: 1.0272x; 1.0272x over previous
import contextlib

import numpy as np
import ml_dtypes

import concourse.bass as bass
import concourse.mybir as mybir
from concourse.bass_utils import run_bass_kernel_spmd

F32 = mybir.dt.float32
BF16 = mybir.dt.bfloat16
U8 = mybir.dt.uint8
I32 = mybir.dt.int32
AF = mybir.ActivationFunctionType
ALU = mybir.AluOpType
AX = mybir.AxisListType

NT = 4096
D = 1024
NH, DH, HP = 16, 64, 8
NG, QW = 8, 512
E, HH, GDIM = 8, 2048, 256
NKC = 32
LN_EPS = 1e-5

ENGINES = ("tensor", "vector", "scalar", "gpsimd", "sync")


class Plan:

    def __init__(self):
        self.q = {e: [] for e in ENGINES}
        self.cnt = {}
        self.lastw = {e: {} for e in ENGINES}

    def val(self, sem):
        return self.cnt.get(id(sem), 0)

    def emit(self, eng, fn, waits=(), inc=None, incby=1):
        w = []
        for sem, v in waits:
            if v is None or v <= 0:
                continue
            k = id(sem)
            if self.lastw[eng].get(k, -1) >= v:
                continue
            self.lastw[eng][k] = v
            w.append((sem, v))

        import traceback as _tb
        _site = "".join(_tb.format_stack(limit=3)[:-1])

        def closure(e, w=tuple(w), fn=fn, inc=inc, incby=incby, _site=_site):
            for sem, v in w:
                e.wait_ge(sem, v)
            try:
                ins = fn(e)
            except Exception:
                print("EMIT SITE:\n", _site)
                raise
            if inc is not None:
                ins.then_inc(inc, incby)

        self.q[eng].append(closure)
        if inc is not None:
            self.cnt[id(inc)] = self.cnt.get(id(inc), 0) + incby
        return self.val(inc) if inc is not None else None


class Arena:

    def __init__(self, nc, kib=206):
        self.h = nc.alloc_sbuf_tensor("arena", [128, kib * 1024], U8)
        self.base = 0
        self.cap = kib * 1024
        self.marks = []

    def alloc(self, dtype, shape_free, align=128):
        esz = {F32: 4, BF16: 2, U8: 1, mybir.dt.int32: 4}[dtype]
        n = 1
        for s in shape_free:
            n *= s
        off = (self.base + align - 1) // align * align
        self.base = off + n * esz
        assert self.base <= self.cap, f"SBUF arena overflow: {self.base} > {self.cap}"
        ap = self.h.ap()[:, off:off + n * esz].bitcast(dtype)
        if len(shape_free) > 1:
            pat = "p (" + " ".join(f"d{i}" for i in range(len(shape_free))) + ") -> p " + \
                  " ".join(f"d{i}" for i in range(len(shape_free)))
            ap = ap.rearrange(pat, **{f"d{i}": s for i, s in enumerate(shape_free)})
        return ap

    def mark(self):
        self.marks.append(self.base)

    def reset(self):
        self.base = self.marks.pop()


RSQ_MAGIC = 0x5F3759DF


def plan_rsqrt(p, S, out, x, rsq, waits=(), eps=0.0):
    w = out.shape[-1]
    xe = rsq[:, 0:w]
    t1 = rsq[:, 8:8 + w]
    t2 = rsq[:, 16:16 + w]
    yi = out.bitcast(I32)
    xei = xe.bitcast(I32)

    def step(fn):
        p.emit("vector", fn, waits=[(S["semV"], p.val(S["semV"]))], inc=S["semV"])

    p.emit("vector", lambda e: e.tensor_scalar(
        out=xe, in0=x, scalar1=eps, scalar2=None, op0=ALU.add),
        waits=list(waits) + [(S["semV"], p.val(S["semV"]))], inc=S["semV"])
    step(lambda e: e.tensor_scalar(out=yi, in0=xei, scalar1=1, scalar2=None,
                                   op0=ALU.arith_shift_right))
    step(lambda e: e.tensor_scalar(out=yi, in0=yi, scalar1=RSQ_MAGIC, scalar2=-1,
                                   op0=ALU.subtract, op1=ALU.mult))
    for _ in range(3):
        step(lambda e: e.tensor_tensor(out=t1, in0=out, in1=out, op=ALU.mult))
        step(lambda e: e.tensor_tensor(out=t2, in0=t1, in1=xe, op=ALU.mult))
        step(lambda e: e.tensor_scalar(out=t2, in0=t2, scalar1=-0.5, scalar2=1.5,
                                       op0=ALU.mult, op1=ALU.add))
        step(lambda e: e.tensor_tensor(out=out, in0=out, in1=t2, op=ALU.mult))
    return p.val(S["semV"])


def bf16_bank(bank):
    return bank[:, :].bitcast(BF16)


def build_program(upto="C", debug=False):
    nc = bass.Bass("TRN2", target_bir_lowering=False)

    x_h = nc.dram_tensor("x", [NT, D], F32, kind="ExternalInput").ap()
    wqkv_h = nc.dram_tensor("wqkv", [D, 3 * D], F32, kind="ExternalInput").ap()
    wg1_h = nc.dram_tensor("wg1", [D, GDIM], F32, kind="ExternalInput").ap()
    bg1_h = nc.dram_tensor("bg1", [GDIM], F32, kind="ExternalInput").ap()
    wg2_h = nc.dram_tensor("wg2", [GDIM, E], F32, kind="ExternalInput").ap()
    win_h = nc.dram_tensor("w_in", [E, D, HH], F32, kind="ExternalInput").ap()
    wout_h = nc.dram_tensor("w_out", [E, HH, D], F32, kind="ExternalInput").ap()
    identb_h = nc.dram_tensor("identb", [128, 128], BF16, kind="ExternalInput").ap()
    identf_h = nc.dram_tensor("identf", [128, 128], F32, kind="ExternalInput").ap()
    ones16_h = nc.dram_tensor("ones16", [128, 128], BF16, kind="ExternalInput").ap()
    onesf_h = nc.dram_tensor("onesf", [128, 128], F32, kind="ExternalInput").ap()
    masks_h = nc.dram_tensor("masks", [128, 4 * QW], BF16, kind="ExternalInput").ap()
    eiota_h = nc.dram_tensor("eiota", [128, 8], F32, kind="ExternalInput").ap()
    tokid_h = nc.dram_tensor("tokid", [128, 8], F32, kind="ExternalInput").ap()
    parange_h = nc.dram_tensor("parange", [128, 1], F32, kind="ExternalInput").ap()
    thr256_h = nc.dram_tensor("thr256", [128, 4], F32, kind="ExternalInput").ap()
    tilest_h = nc.dram_tensor("tilest", [128, 16], F32, kind="ExternalInput").ap()

    out_h = nc.dram_tensor("out", [NT, D], F32, kind="ExternalOutput").ap()

    skind = "ExternalOutput" if debug else "Internal"
    q_scr = nc.dram_tensor("q_scr", [HP, 128, NT], BF16, kind=skind).ap()
    k_scr = nc.dram_tensor("k_scr", [HP, 128, NT], BF16, kind=skind).ap()
    v_scr = nc.dram_tensor("v_scr", [NKC, 128, NH * 65], BF16, kind=skind).ap()
    cb_scr = nc.dram_tensor("cb_scr", [128, 128], F32, kind=skind).ap()
    dbgC = nc.dram_tensor("dbgC", [128, 128], F32, kind="ExternalOutput" if debug else "Internal").ap()
    xp_scr = nc.dram_tensor("xp_scr", [NT, D], F32, kind=skind).ap()
    xn2_scr = nc.dram_tensor("xn2_scr", [8, 128, NT], BF16, kind=skind).ap()
    wg1_scr = nc.dram_tensor("wg1_scr", [128, 8 * GDIM], BF16, kind="Internal").ap()
    wg2_scr = nc.dram_tensor("wg2_scr", [128, 2 * E], BF16, kind="Internal").ap()
    wi_scr = nc.dram_tensor("wi_scr", [E, 128, 8 * HH], BF16, kind="Internal").ap()
    wo_scr = nc.dram_tensor("wo_scr", [E, 128, 16 * D], BF16, kind="Internal").ap()
    xn2tm_scr = nc.dram_tensor("xn2tm_scr", [NT, D], BF16, kind=skind).ap()
    stream_scr = [nc.dram_tensor(f"stream_scr{i}", [3840], mybir.dt.int32,
                                 kind=skind).ap() for i in range(2)]
    gates_scr = [nc.dram_tensor(f"gates_scr{i}", [3840], F32,
                                kind=skind).ap() for i in range(2)]
    o_scr = [nc.dram_tensor(f"o_scr{i}", [3840, D], F32,
                            kind=skind).ap() for i in range(2)]

    p = Plan()

    with contextlib.ExitStack() as ctx:
        en = ctx.enter_context
        ar = Arena(nc, kib=206)
        sAp = en(nc.psum_tensor("sA", [128, 1024], F32))
        sBp = en(nc.psum_tensor("sB", [128, 1024], F32))
        sCp = en(nc.psum_tensor("sC", [128, 1024], F32))
        pv0 = en(nc.psum_tensor("pv0", [128, 512], F32))
        pv1 = en(nc.psum_tensor("pv1", [128, 512], F32))
        pb = [sAp[:, 0:512], sAp[:, 512:1024], sBp[:, 0:512], sBp[:, 512:1024],
              sCp[:, 0:512], sCp[:, 512:1024], pv0[:, :], pv1[:, :]]
        bankpair = [sAp[:, :], sBp[:, :], sCp[:, :]]

        sem_names = [
            "semT", "semV", "semA",
            "cqkv", "cwg", "cwin", "cwout", "ldc",
            "xldA", "qkst0", "qkst1", "vst0", "vst1",
            "kld", "vld", "qld0", "qld1",
            "xldB0", "xldB1", "xpst0", "xpst1", "xnst", "cbst",
            "xcld0", "xcld1", "xpldC0", "xpldC1",
            "outst0", "outst1", "gld", "xnts",
            "wgat", "xgat0", "xgat1", "scat", "strz", "strld0", "strld1",
            "ygat", "ygat1", "ostr0", "ostr1",
        ]
        S = {n: en(nc.semaphore(n)) for n in sem_names}

        identb = ar.alloc(BF16, [128])
        identf = ar.alloc(F32, [128])
        ones16 = ar.alloc(BF16, [128])
        onesf = ar.alloc(F32, [128])
        stat = ar.alloc(F32, [192])
        cbias = ar.alloc(F32, [128])
        bgt = ar.alloc(F32, [2])
        cvals = ar.alloc(F32, [4])
        rsq = ar.alloc(F32, [24])
        eiota = ar.alloc(F32, [8])
        tokid = ar.alloc(F32, [8])
        parange = ar.alloc(F32, [1])
        thr256 = ar.alloc(F32, [4])
        tilest = ar.alloc(F32, [16])

        p.emit("vector", lambda e: e.memset(cvals[:, 0:1], LN_EPS), inc=S["semV"])
        p.emit("vector", lambda e: e.memset(cvals[:, 1:2], -8.0), inc=S["semV"])
        p.emit("vector", lambda e: e.memset(cvals[:, 2:3], -0.5), inc=S["semV"])
        CV = p.val(S["semV"])

        for i, (dst, src) in enumerate(((identb, identb_h), (identf, identf_h),
                                        (ones16, ones16_h), (onesf, onesf_h))):
            p.emit("sync", lambda e, d=dst, s=src: e.dma_start(out=d, in_=s),
                   waits=[(S["ldc"], 16 * i)], inc=S["ldc"], incby=16)
        p.emit("sync", lambda e: e.dma_start(
            out=bgt, in_=bg1_h.rearrange("(c p) -> p c", p=128),
            allow_slow_non_contiguous=True),
            waits=[(S["ldc"], 64)], inc=S["ldc"], incby=16)
        for dst, srch in ((eiota, eiota_h), (tokid, tokid_h), (parange, parange_h),
                          (thr256, thr256_h), (tilest, tilest_h)):
            prev = p.val(S["ldc"])
            p.emit("sync", lambda e, d=dst, s=srch: e.dma_start(out=d, in_=s),
                   waits=[(S["ldc"], prev)], inc=S["ldc"], incby=16)
        LDC = dict(identb=16, identf=32, ones16=48, onesf=64, bg1=80,
                   eiota=96, tokid=112, parange=128, thr256=144, tilest=160,
                   masks=176)

        ar.mark()
        wqkv_sb = ar.alloc(BF16, [8, 3 * D])
        xA2 = ar.alloc(F32, [2, 4, D])
        xn1_tm2 = ar.alloc(BF16, [2, 4, D])
        xn1_fm2 = ar.alloc(BF16, [2, 8, QW])
        bnb2 = ar.alloc(F32, [2, 4, 12])
        mv2 = ar.alloc(F32, [2, 4, 2])
        rstdA2 = ar.alloc(F32, [2, 4])
        nmrsA2 = ar.alloc(F32, [2, 4])
        qkst = ar.alloc(BF16, [2, QW])
        sqb = ar.alloc(BF16, [2, QW])
        vstg = ar.alloc(BF16, [2, NH, 65])

        p.emit("gpsimd", lambda e: e.dma_start(
            out=wqkv_sb, in_=wqkv_h.rearrange("(k p) m -> p k m", p=128)),
            inc=S["cqkv"], incby=16)
        p.emit("gpsimd", lambda e: e.dma_start(
            out=wg1_scr.rearrange("p (k m) -> p k m", k=8, m=GDIM),
            in_=wg1_h.rearrange("(k p) m -> p k m", p=128)),
            waits=[(S["cqkv"], 16)], inc=S["cwg"], incby=16)
        p.emit("gpsimd", lambda e: e.dma_start(
            out=wg2_scr.rearrange("p (k m) -> p k m", k=2, m=E),
            in_=wg2_h.rearrange("(k p) m -> p k m", p=128)),
            waits=[(S["cwg"], 16)], inc=S["cwg"], incby=16)
        for ei in range(E):
            p.emit("gpsimd", lambda e, ei=ei: e.dma_start(
                out=wi_scr[ei].rearrange("p (k h) -> p k h", k=8, h=HH),
                in_=win_h[ei].rearrange("(k p) h -> p k h", p=128)),
                waits=[(S["cwg"], 32), (S["cwin"], 16 * ei)], inc=S["cwin"], incby=16)
        for ei in range(E):
            p.emit("gpsimd", lambda e, ei=ei: e.dma_start(
                out=wo_scr[ei].rearrange("p (k d) -> p k d", k=16, d=D),
                in_=wout_h[ei].rearrange("(k p) d -> p k d", p=128)),
                waits=[(S["cwin"], 16 * E), (S["cwout"], 16 * ei)],
                inc=S["cwout"], incby=16)

        p.emit("vector", lambda e: e.memset(vstg[:, :, :, 64:65], 1.0), inc=S["semV"])

        qk_stg = [None, None]
        qk_psg = [0, 0]
        v_stg = [0, 0]
        v_psg = [0, 0]
        evacA = [0, 0, 0, 0]

        xaA_war = [0, 0]
        xaV_war = [0, 0]
        for g in range(NG):
            sg = g % 2
            xA = xA2[:, sg]
            xn1_tm = xn1_tm2[:, sg]
            xn1_fm = xn1_fm2[:, sg]
            bnb = bnb2[:, sg]
            mv = mv2[:, sg]
            rstdA = rstdA2[:, sg]
            nmrsA = nmrsA2[:, sg]
            p.emit("sync", lambda e, g=g, xA=xA: e.dma_start(
                out=xA, in_=x_h[g * QW:(g + 1) * QW, :].rearrange(
                    "(j p) d -> p j d", p=128)),
                waits=[(S["semA"], xaA_war[sg]), (S["semV"], xaV_war[sg])],
                inc=S["xldA"], incby=16)
            xld = p.val(S["xldA"])

            for j in range(4):
                p.emit("vector", lambda e, j=j, bnb=bnb, xA=xA: e.bn_stats(
                    bnb[:, j, 0:6], xA[:, j, 0:512]),
                    waits=[(S["xldA"], xld)])
                p.emit("vector", lambda e, j=j, bnb=bnb, xA=xA: e.bn_stats(
                    bnb[:, j, 6:12], xA[:, j, 512:1024]),
                    inc=S["semV"] if j == 3 else None)
            vBn = p.val(S["semV"])
            for j in range(4):
                p.emit("vector", lambda e, j=j, mv=mv, bnb=bnb: e.bn_aggr(
                    mv[:, j, :], bnb[:, j, :]),
                    waits=[(S["semV"], vBn)], inc=S["semV"] if j == 3 else None)
            vStats = p.val(S["semV"])
            p.emit("scalar", lambda e, rstdA=rstdA, mv=mv: e.activation(
                out=rstdA, in_=mv[:, :, 1],
                func=AF.Ln, bias=cvals[:, 0:1], scale=1.0),
                waits=[(S["semV"], vStats), (S["semV"], CV)], inc=S["semA"])
            aLn = p.val(S["semA"])
            p.emit("scalar", lambda e, rstdA=rstdA: e.activation(
                out=rstdA, in_=rstdA, func=AF.Exp, bias=0.0, scale=-0.5),
                waits=[(S["semA"], aLn)], inc=S["semA"])
            aRstd = p.val(S["semA"])
            p.emit("vector", lambda e, nmrsA=nmrsA, mv=mv, rstdA=rstdA:
                e.scalar_tensor_tensor(
                    out=nmrsA, in0=mv[:, :, 0], scalar=-1.0, in1=rstdA,
                    op0=ALU.mult, op1=ALU.mult),
                waits=[(S["semA"], aRstd)], inc=S["semV"])
            vNm = p.val(S["semV"])

            for j in range(4):
                p.emit("scalar", lambda e, j=j, xn1_tm=xn1_tm, xA=xA,
                       nmrsA=nmrsA, rstdA=rstdA: e.activation(
                    out=xn1_tm[:, j, :], in_=xA[:, j, :], func=AF.Identity,
                    bias=nmrsA[:, j:j + 1], scale=rstdA[:, j:j + 1]),
                    waits=[(S["semV"], vNm)], inc=S["semA"])
                aApp = p.val(S["semA"])
                bank = pb[6 + (j % 2)]
                bview = bf16_bank(bank).rearrange("p (f c) -> p f c", f=8, c=128)
                for f in range(8):
                    p.emit("tensor", lambda e, j=j, f=f, bview=bview,
                           xn1_tm=xn1_tm: e.transpose(
                        bview[:, f, :], xn1_tm[:, j, 128 * f:128 * (f + 1)], identb),
                        waits=[(S["semA"], aApp), (S["semA"], evacA[j - 2] if j >= 2 else 0),
                               (S["ldc"], LDC["identb"])],
                        inc=S["semT"] if f == 7 else None)
                tT = p.val(S["semT"])
                p.emit("scalar", lambda e, j=j, bview=bview, xn1_fm=xn1_fm: e.copy(
                    out=xn1_fm[:, :, 128 * j:128 * (j + 1)], in_=bview[:, :, :]),
                    waits=[(S["semT"], tT)], inc=S["semA"])
                evacA[j] = p.val(S["semA"])
            aFm = evacA[3]
            xaA_war[sg] = p.val(S["semA"])
            xaV_war[sg] = vStats

            pend_stats = []
            sq_war = [0, 0]
            stat_war = [0]

            def emit_stats(m, slot, vSq, g=g):
                is_q = m < 8
                for par in range(2):
                    sb = pb[4 + par]
                    p.emit("tensor", lambda e, par=par, sb=sb, slot=slot: e.matmul(
                        sb[0:1, :], ones16[64 * par:64 * par + 64, 0:1],
                        sqb[64 * par:64 * par + 64, slot, :], start=True, stop=True),
                        waits=[(S["semV"], vSq), (S["semV"], stat_war[0]),
                               (S["ldc"], LDC["ones16"])],
                        inc=S["semT"])
                tStat = p.val(S["semT"])
                for par in range(2):
                    sb = pb[4 + par]
                    h = 2 * (m % 8) + par
                    if is_q:
                        dst = stat[0:1, h * 8 + g:h * 8 + g + 1]
                        p.emit("vector", lambda e, sb=sb, dst=dst: e.reduce_max(
                            dst, sb[0:1, :], axis=AX.X),
                            waits=[(S["semT"], tStat)], inc=S["semV"])
                    elif g == 0:
                        dst = stat[0:1, 128 + h:128 + h + 1]
                        p.emit("vector", lambda e, sb=sb, dst=dst: e.reduce_max(
                            dst, sb[0:1, :], axis=AX.X),
                            waits=[(S["semT"], tStat)], inc=S["semV"])
                    else:
                        tmp = stat[0:1, 144 + par:145 + par]
                        dst = stat[0:1, 128 + h:128 + h + 1]
                        p.emit("vector", lambda e, sb=sb, tmp=tmp: e.reduce_max(
                            tmp, sb[0:1, :], axis=AX.X),
                            waits=[(S["semT"], tStat)], inc=S["semV"])
                        vR = p.val(S["semV"])
                        p.emit("vector", lambda e, tmp=tmp, dst=dst: e.tensor_tensor(
                            out=dst, in0=dst, in1=tmp, op=ALU.max),
                            waits=[(S["semV"], vR)], inc=S["semV"])
                sq_war[slot] = p.val(S["semV"])
                stat_war[0] = p.val(S["semV"])

            for m in range(16):
                is_q = m < 8
                slot = m % 2
                bank = pb[slot]
                for k in range(8):
                    p.emit("tensor", lambda e, m=m, k=k, bank=bank,
                           xn1_fm=xn1_fm: e.matmul(
                        bank[:, :], wqkv_sb[:, k, 128 * m:128 * (m + 1)],
                        xn1_fm[:, k, :], start=(k == 0), stop=(k == 7)),
                        waits=[(S["semA"], aFm), (S["cqkv"], 16),
                               (S["semA"], qk_psg[slot])],
                        inc=S["semT"] if k == 7 else None)
                tQK = p.val(S["semT"])
                ew = [(S["semT"], tQK)]
                if qk_stg[slot] is not None:
                    ew.append(qk_stg[slot])
                p.emit("scalar", lambda e, slot=slot, bank=bank: e.copy(
                    out=qkst[:, slot, :], in_=bank[:, :]),
                    waits=ew, inc=S["semA"])
                aEvac = p.val(S["semA"])
                qk_psg[slot] = aEvac
                scr = q_scr if is_q else k_scr
                ssem = S[f"qkst{slot}"]
                hp = m % 8
                p.emit("sync", lambda e, hp=hp, g=g, slot=slot, scr=scr: e.dma_start(
                    out=scr[hp][:, g * QW:(g + 1) * QW], in_=qkst[:, slot, :]),
                    waits=[(S["semA"], aEvac)], inc=ssem, incby=16)
                qk_stg[slot] = (ssem, p.val(ssem))
                p.emit("vector", lambda e, slot=slot: e.tensor_tensor(
                    out=sqb[:, slot, :], in0=qkst[:, slot, :], in1=qkst[:, slot, :],
                    op=ALU.mult),
                    waits=[(S["semA"], aEvac), (S["semV"], sq_war[slot])],
                    inc=S["semV"])
                vSq = p.val(S["semV"])
                pend_stats.append((m, slot, vSq))
                if len(pend_stats) >= 2 or m == 15:
                    todo = pend_stats if m == 15 else pend_stats[:1]
                    for (pm, pslot, pvSq) in todo:
                        emit_stats(pm, pslot, pvSq)
                    del pend_stats[:len(todo)]

            for j in range(4):
                slot = j % 2
                bank = pb[2 + slot]
                for n in range(2):
                    for k in range(8):
                        p.emit("tensor", lambda e, j=j, n=n, k=k, bank=bank,
                               xn1_fm=xn1_fm: e.matmul(
                            bank[:, :], xn1_fm[:, k, 128 * j:128 * (j + 1)],
                            wqkv_sb[:, k, 2 * D + 512 * n: 2 * D + 512 * (n + 1)],
                            start=(k == 0), stop=(k == 7)),
                            waits=[(S["semA"], aFm), (S["semV"], v_psg[slot])],
                            inc=S["semT"] if k == 7 else None)
                    tV = p.val(S["semT"])
                    bview = bank[:, :].rearrange("p (a b) -> p a b", a=8, b=64)
                    p.emit("vector", lambda e, slot=slot, n=n, bview=bview: e.tensor_copy(
                        out=vstg[:, slot, 8 * n:8 * (n + 1), 0:64], in_=bview),
                        waits=[(S["semT"], tV), (S[f"vst{slot}"], v_stg[slot])],
                        inc=S["semV"])
                    v_psg[slot] = p.val(S["semV"])
                vEv = p.val(S["semV"])
                kc = 4 * g + j
                p.emit("sync", lambda e, kc=kc, slot=slot: e.dma_start(
                    out=v_scr[kc].rearrange("p (a b) -> p a b", a=NH, b=65),
                    in_=vstg[:, slot, :, :]),
                    waits=[(S["semV"], vEv)], inc=S[f"vst{slot}"], incby=16)
                v_stg[slot] = p.val(S[f"vst{slot}"])

        vAll = p.val(S["semV"])
        sq_q = stat[0:1, 0:128]
        sq_k = stat[0:1, 128:144]
        p.emit("scalar", lambda e: e.activation(out=sq_q, in_=sq_q, func=AF.Sqrt,
                                                bias=0.0, scale=1.0),
               waits=[(S["semV"], vAll)], inc=S["semA"])
        p.emit("scalar", lambda e: e.activation(out=sq_k, in_=sq_k, func=AF.Sqrt,
                                                bias=0.0, scale=1.0),
               waits=[(S["semA"], p.val(S["semA"]))], inc=S["semA"])
        aSq = p.val(S["semA"])
        for h in range(NH):
            p.emit("vector", lambda e, h=h: e.tensor_single_scalar(
                out=sq_q[0:1, 8 * h:8 * (h + 1)], in_=sq_q[0:1, 8 * h:8 * (h + 1)],
                scalar=sq_k[0:1, h:h + 1], op=ALU.mult),
                waits=[(S["semA"], aSq)], inc=S["semV"] if h == NH - 1 else None)
        vCm = p.val(S["semV"])
        p.emit("scalar", lambda e: e.activation(out=sq_q, in_=sq_q, func=AF.Identity,
                                                bias=cvals[0:1, 2:3], scale=-0.125 * 1.02),
               waits=[(S["semV"], vCm)], inc=S["semA"])
        aNegc = p.val(S["semA"])
        p.emit("tensor", lambda e: e.matmul(pb[6][:, 0:128], onesf[0:1, :], sq_q,
                                            start=True, stop=True),
               waits=[(S["semA"], aNegc), (S["ldc"], LDC["onesf"])], inc=S["semT"])
        tCb = p.val(S["semT"])
        p.emit("vector", lambda e: e.tensor_copy(out=cbias, in_=pb[6][:, 0:128]),
               waits=[(S["semT"], tCb)], inc=S["semV"])
        vCb = p.val(S["semV"])
        if debug:
            p.emit("sync", lambda e: e.dma_start(out=cb_scr, in_=cbias),
                   waits=[(S["semV"], vCb)], inc=S["cbst"], incby=16)

        A = dict(T=p.val(S["semT"]), V=p.val(S["semV"]), A=p.val(S["semA"]),
                 qk0=p.val(S["qkst0"]), qk1=p.val(S["qkst1"]),
                 v0=p.val(S["vst0"]), v1=p.val(S["vst1"]))
        ar.reset()

        if upto == "A":
            p.emit("sync", lambda e: e.nop(), waits=_dma_ends(p, S))
            _finalize(nc, p)
            return nc

        B = _phase_b(nc, p, S, ar, pb, bankpair, x_h, q_scr, k_scr, v_scr, masks_h,
                     xp_scr, xn2_scr, xn2tm_scr, identb, identf, cbias, cvals, rsq,
                     A, LDC)

        if upto == "B":
            p.emit("sync", lambda e: e.nop(), waits=_dma_ends(p, S))
            _finalize(nc, p)
            return nc

        C = _phase_c(nc, p, S, ar, pb, xp_scr, xn2_scr, xn2tm_scr, wg1_scr, wg2_scr,
                     wi_scr, wo_scr, stream_scr, gates_scr, o_scr, bgt, cvals, rsq,
                     identb, identf, onesf, eiota, tokid, parange, thr256, tilest,
                     out_h, B, LDC, dbgC)
        p.emit("sync", lambda e: e.nop(), waits=_dma_ends(p, S))
        _finalize(nc, p)
        return nc



def _dma_ends(p, S):
    ends = []
    for n, sem in S.items():
        if n in ("semT", "semV", "semA"):
            continue
        v = p.val(sem)
        if v > 0:
            ends.append((sem, v))
    return ends

def _finalize(nc, p):
    with nc.Block() as block:
        @block.tensor
        def _(e):
            for f in p.q["tensor"]:
                f(e)

        @block.vector
        def _(e):
            for f in p.q["vector"]:
                f(e)

        @block.scalar
        def _(e):
            for f in p.q["scalar"]:
                f(e)

        @block.gpsimd
        def _(e):
            for f in p.q["gpsimd"]:
                f(e)

        @block.sync
        def _(e):
            for f in p.q["sync"]:
                f(e)


def _phase_b(nc, p, S, ar, pb, bankpair, x_h, q_scr, k_scr, v_scr, masks_h,
             xp_scr, xn2_scr, xn2tm_scr, identb, identf, cbias, cvals, rsq, A, LDC):
    ar.mark()
    kbuf = ar.alloc(BF16, [HP, NT])
    vbuf = ar.alloc(BF16, [NKC, NH, 65])
    qbuf = ar.alloc(BF16, [2, HP, QW])
    masks = ar.alloc(BF16, [4, QW])
    pT = ar.alloc(BF16, [3, 2 * QW])
    pvb = ar.alloc(F32, [QW])
    attn_tm = ar.alloc(F32, [4, D])
    xB = ar.alloc(F32, [2, D])
    xpB = ar.alloc(F32, [2, D])
    xn2_tm = ar.alloc(BF16, [D])
    xn2_st = ar.alloc(BF16, [8, QW])
    rcol = ar.alloc(F32, [8])
    bnbB = ar.alloc(F32, [4, 12])
    mvB = ar.alloc(F32, [4, 2])
    rstdB = ar.alloc(F32, [8])
    nmrsB = ar.alloc(F32, [8])

    bwaits = [(S["semT"], A["T"]), (S["semV"], A["V"]), (S["semA"], A["A"]),
              (S["qkst0"], A["qk0"]), (S["qkst1"], A["qk1"]),
              (S["vst0"], A["v0"]), (S["vst1"], A["v1"])]
    p.emit("sync", lambda e: e.dma_start(
        out=kbuf, in_=k_scr.rearrange("hp p t -> p hp t")),
        waits=bwaits, inc=S["kld"], incby=16)
    p.emit("sync", lambda e: e.dma_start(
        out=vbuf, in_=v_scr.rearrange("kc p (h o) -> p kc h o", h=NH, o=65)),
        inc=S["vld"], incby=16)
    p.emit("sync", lambda e: e.dma_start(
        out=masks, in_=masks_h.rearrange("p (d w) -> p d w", d=4, w=QW)),
        waits=[(S["ldc"], 160)], inc=S["ldc"], incby=16)

    q_guard = [0, 0]
    x_guard = [0, 0]
    xp_guard = [0, 0]
    xn_guard = 0
    xnts_guard = [0]
    apply_guard = [0, 0]
    tr_guard = [0, 0]
    prev_epi_A = 0

    pending_epi = [None]
    for qt in range(NG):
        qsem = S["qld0"] if qt % 2 == 0 else S["qld1"]
        slotq = qt % 2
        p.emit("sync", lambda e, qt=qt, slotq=slotq: e.dma_start(
            out=qbuf[:, slotq, :, :],
            in_=q_scr[:, :, qt * QW:(qt + 1) * QW].rearrange("hp p t -> p hp t")),
            waits=[(S["semT"], q_guard[slotq])], inc=qsem, incby=16)
        qldv = p.val(qsem)
        nkc = 4 * qt + 4

        for h in range(NH):
            hp, par = h // 2, h % 2
            pvbank = pb[6 + h % 2]
            npair = nkc // 2
            exp_cnt = []
            pv_cnt = []
            sc_cnt = []

            def emit_scores(pr):
                g3 = pr % 3
                sp = bankpair[g3]
                for i in range(2):
                    kc = 2 * pr + i
                    w = [(qsem, qldv), (S["kld"], 16), (S["vld"], 16),
                         (S["semA"], prev_epi_A)]
                    if pr >= 3:
                        w.append((S["semA"], exp_cnt[pr - 3]))
                    p.emit("tensor", lambda e, hp=hp, par=par, kc=kc, i=i,
                           sp=sp, slotq=slotq: e.matmul(
                        sp[:, 512 * i:512 * (i + 1)],
                        kbuf[64 * par:64 * par + 64, hp, 128 * kc:128 * (kc + 1)],
                        qbuf[64 * par:64 * par + 64, slotq, hp, :],
                        start=True, stop=True),
                        waits=w, inc=S["semT"] if i == 1 else None)
                sc_cnt.append(p.val(S["semT"]))

            def emit_exp(pr):
                g3 = pr % 3
                sp = bankpair[g3]
                ew = [(S["semT"], sc_cnt[pr]), (S["semV"], A["V"])]
                if 2 * pr + 1 >= 4 * qt:
                    d = 2 * (pr - (npair - 2))
                    p.emit("vector", lambda e, sp=sp, d=d: e.tensor_tensor(
                        out=sp.rearrange("p (a b) -> p a b", a=2, b=512),
                        in0=sp.rearrange("p (a b) -> p a b", a=2, b=512),
                        in1=masks[:, d:d + 2, :], op=ALU.add),
                        waits=[(S["semT"], sc_cnt[pr]), (S["ldc"], LDC["masks"])],
                        inc=S["semV"])
                    ew = [(S["semV"], p.val(S["semV"])), (S["semV"], A["V"])]
                if pr >= 3:
                    ew.append((S["semT"], pv_cnt[pr - 3]))
                p.emit("scalar", lambda e, sp=sp, g3=g3, h=h, qt=qt: e.activation(
                    out=pT[:, g3, :], in_=sp, func=AF.Exp,
                    bias=cbias[:, h * 8 + qt:h * 8 + qt + 1], scale=0.125),
                    waits=ew, inc=S["semA"])
                exp_cnt.append(p.val(S["semA"]))

            def emit_pv(pr):
                g3 = pr % 3
                w = [(S["semA"], exp_cnt[pr])]
                if pr == 0:
                    w.append((S["semV"], tr_guard[h % 2]))
                for i in range(2):
                    kc = 2 * pr + i
                    p.emit("tensor", lambda e, kc=kc, h=h, g3=g3, i=i,
                           pvbank=pvbank, nkc=nkc: e.matmul(
                        pvbank[0:65, :], vbuf[:, kc, h, :],
                        pT[:, g3, 512 * i:512 * (i + 1)],
                        start=(kc == 0), stop=(kc == nkc - 1)),
                        waits=w if i == 0 else [(S["semA"], exp_cnt[pr])],
                        inc=S["semT"] if i == 1 else None)
                pv_cnt.append(p.val(S["semT"]))

            for pr in range(npair):
                emit_scores(pr)
                emit_exp(pr)
                if pr == 0 and pending_epi[0] is not None:
                    pending_epi[0]()
                    pending_epi[0] = None
                if pr >= 2:
                    emit_pv(pr - 2)
            emit_pv(npair - 2) if npair >= 2 else None
            emit_pv(npair - 1)

            def make_epi(h=h, pvbank=pvbank, tPV=None):
                tPV = pv_cnt[-1]

                def epi():
                    p.emit("scalar", lambda e, pvbank=pvbank: e.copy(
                        out=pvb[0:65, :], in_=pvbank[0:65, :]),
                        waits=[(S["semT"], tPV)], inc=S["semA"])
                    aPv = p.val(S["semA"])
                    trbank = pvbank
                    for j in range(4):
                        p.emit("tensor", lambda e, j=j, trbank=trbank: e.transpose(
                            trbank[:, 65 * j:65 * (j + 1)],
                            pvb[0:65, 128 * j:128 * (j + 1)],
                            identf[0:65, 0:65]),
                            waits=[(S["semA"], aPv), (S["ldc"], LDC["identf"])],
                            inc=S["semT"] if j == 3 else None)
                    tTr = p.val(S["semT"])
                    trv = trbank[:, 0:260].rearrange("p (a b) -> p a b", a=4, b=65)
                    p.emit("vector", lambda e, trv=trv: e.reciprocal(
                        rcol[:, 0:4], trv[:, :, 64]),
                        waits=[(S["semT"], tTr)], inc=S["semV"])
                    vRc = p.val(S["semV"])
                    for j in range(4):
                        p.emit("vector", lambda e, j=j, h=h, trv=trv: e.tensor_scalar(
                            out=attn_tm[:, j, 64 * h:64 * (h + 1)], in0=trv[:, j, 0:64],
                            scalar1=rcol[:, j:j + 1], scalar2=None, op0=ALU.mult),
                            waits=[(S["semV"], vRc)],
                            inc=S["semV"] if j == 3 else None)
                    tr_guard[h % 2] = p.val(S["semV"])
                return epi

            pending_epi[0] = make_epi()
        pending_epi[0]()
        pending_epi[0] = None
        q_guard[slotq] = p.val(S["semT"])
        vHeads = p.val(S["semV"])

        for j in range(4):
            p.emit("vector", lambda e, j=j: e.bn_stats(
                bnbB[:, j, 0:6], attn_tm[:, j, 0:512]),
                waits=[(S["semV"], vHeads), (S["semA"], prev_epi_A)])
            p.emit("vector", lambda e, j=j: e.bn_stats(
                bnbB[:, j, 6:12], attn_tm[:, j, 512:1024]),
                inc=S["semV"] if j == 3 else None)
        vBn2 = p.val(S["semV"])
        for j in range(4):
            p.emit("vector", lambda e, j=j: e.bn_aggr(mvB[:, j, :], bnbB[:, j, :]),
                   waits=[(S["semV"], vBn2)], inc=S["semV"] if j == 3 else None)
        vSt = p.val(S["semV"])
        p.emit("scalar", lambda e: e.activation(out=rstdB[:, 0:4], in_=mvB[:, :, 1],
                                                func=AF.Ln, bias=cvals[:, 0:1], scale=1.0),
               waits=[(S["semV"], vSt)], inc=S["semA"])
        p.emit("scalar", lambda e: e.activation(out=rstdB[:, 0:4], in_=rstdB[:, 0:4],
                                                func=AF.Exp, bias=0.0, scale=-0.5),
               waits=[(S["semA"], p.val(S["semA"]))], inc=S["semA"])
        aR = p.val(S["semA"])
        p.emit("vector", lambda e: e.scalar_tensor_tensor(
            out=nmrsB[:, 0:4], in0=mvB[:, :, 0], scalar=-1.0, in1=rstdB[:, 0:4],
            op0=ALU.mult, op1=ALU.mult),
            waits=[(S["semA"], aR)], inc=S["semV"])
        vNm = p.val(S["semV"])

        last_add = 0
        tXn_prev = 0
        for jp in range(2):
            for j in (2 * jp, 2 * jp + 1):
                slot = j % 2
                xsem = S["xldB0"] if slot == 0 else S["xldB1"]
                p.emit("sync", lambda e, qt=qt, j=j, slot=slot: e.dma_start(
                    out=xB[:, slot, :],
                    in_=x_h[qt * QW + 128 * j: qt * QW + 128 * (j + 1), :]),
                    waits=[(S["semV"], x_guard[slot])], inc=xsem, incby=16)
                xw = p.val(xsem)
                p.emit("scalar", lambda e, j=j: e.activation(
                    out=attn_tm[:, j, :], in_=attn_tm[:, j, :], func=AF.Identity,
                    bias=nmrsB[:, j:j + 1], scale=rstdB[:, j:j + 1]),
                    waits=[(S["semV"], vNm), (S["semV"], last_add)], inc=S["semA"])
                aL = p.val(S["semA"])
                p.emit("vector", lambda e, slot=slot, j=j: e.tensor_add(
                    out=xpB[:, slot, :], in0=xB[:, slot, :], in1=attn_tm[:, j, :]),
                    waits=[(S["semA"], aL), (xsem, xw),
                           (S[f"xpst{slot}"], xp_guard[slot]),
                           (S["semA"], apply_guard[slot])],
                    inc=S["semV"])
                last_add = p.val(S["semV"])
                x_guard[slot] = last_add
                vXp = last_add
                p.emit("sync", lambda e, qt=qt, j=j, slot=slot: e.dma_start(
                    out=xp_scr[qt * QW + 128 * j: qt * QW + 128 * (j + 1), :],
                    in_=xpB[:, slot, :]),
                    waits=[(S["semV"], vXp)], inc=S[f"xpst{slot}"], incby=16)
                xp_guard[slot] = p.val(S[f"xpst{slot}"])
                p.emit("vector", lambda e, slot=slot, j=j: e.bn_stats(
                    bnbB[:, j, 0:6], xpB[:, slot, 0:512]), waits=[(S["semV"], vXp)])
                p.emit("vector", lambda e, slot=slot, j=j: e.bn_stats(
                    bnbB[:, j, 6:12], xpB[:, slot, 512:1024]), inc=S["semV"])
                vBn3 = p.val(S["semV"])
                p.emit("vector", lambda e, j=j: e.bn_aggr(mvB[:, j, :], bnbB[:, j, :]),
                       waits=[(S["semV"], vBn3)], inc=S["semV"])
            vAggr2 = p.val(S["semV"])
            c0 = 4 + 2 * jp
            p.emit("scalar", lambda e, jp=jp, c0=c0: e.activation(
                out=rstdB[:, c0:c0 + 2], in_=mvB[:, 2 * jp:2 * jp + 2, 1],
                func=AF.Ln, bias=cvals[:, 0:1], scale=1.0),
                waits=[(S["semV"], vAggr2)], inc=S["semA"])
            p.emit("scalar", lambda e, c0=c0: e.activation(
                out=rstdB[:, c0:c0 + 2], in_=rstdB[:, c0:c0 + 2], func=AF.Exp,
                bias=0.0, scale=-0.5),
                waits=[(S["semA"], p.val(S["semA"]))], inc=S["semA"])
            aR2 = p.val(S["semA"])
            p.emit("vector", lambda e, jp=jp, c0=c0: e.scalar_tensor_tensor(
                out=nmrsB[:, c0:c0 + 2], in0=mvB[:, 2 * jp:2 * jp + 2, 0],
                scalar=-1.0, in1=rstdB[:, c0:c0 + 2], op0=ALU.mult, op1=ALU.mult),
                waits=[(S["semA"], aR2)], inc=S["semV"])
            vN2 = p.val(S["semV"])
            for j in (2 * jp, 2 * jp + 1):
                slot = j % 2
                p.emit("scalar", lambda e, slot=slot, j=j, jp=jp: e.activation(
                    out=xn2_tm, in_=xpB[:, slot, :], func=AF.Identity,
                    bias=nmrsB[:, 4 + j:5 + j], scale=rstdB[:, 4 + j:5 + j]),
                    waits=[(S["semV"], vN2), (S["xnts"], xnts_guard[0]),
                           (S["semT"], tXn_prev)], inc=S["semA"])
                aXn = p.val(S["semA"])
                apply_guard[slot] = aXn
                p.emit("sync", lambda e, qt=qt, j=j: e.dma_start(
                    out=xn2tm_scr[qt * QW + 128 * j: qt * QW + 128 * (j + 1), :],
                    in_=xn2_tm),
                    waits=[(S["semA"], aXn)], inc=S["xnts"], incby=16)
                xnts_guard[0] = p.val(S["xnts"])
                tb = pb[6 + (j % 2)]
                tbv = bf16_bank(tb).rearrange("p (f c) -> p f c", f=8, c=128)
                for f in range(8):
                    p.emit("tensor", lambda e, f=f, tbv=tbv: e.transpose(
                        tbv[:, f, :], xn2_tm[:, 128 * f:128 * (f + 1)], identb),
                        waits=[(S["semA"], aXn)],
                        inc=S["semT"] if f == 7 else None)
                tXn = p.val(S["semT"])
                tXn_prev = tXn
                p.emit("scalar", lambda e, j=j, tbv=tbv: e.copy(
                    out=xn2_st[:, :, 128 * j:128 * (j + 1)], in_=tbv[:, :, :]),
                    waits=[(S["semT"], tXn), (S["xnst"], xn_guard)], inc=S["semA"])
        aXnAll = p.val(S["semA"])
        p.emit("sync", lambda e, qt=qt: e.dma_start(
            out=xn2_scr[:, :, qt * QW:(qt + 1) * QW].rearrange("f p t -> p f t"),
            in_=xn2_st[:, :, :]),
            waits=[(S["semA"], aXnAll)], inc=S["xnst"], incby=16)
        xn_guard = p.val(S["xnst"])
        prev_epi_A = aXnAll

    res = dict(T=p.val(S["semT"]), V=p.val(S["semV"]), A=p.val(S["semA"]),
               xpst0=p.val(S["xpst0"]), xpst1=p.val(S["xpst1"]),
               xnst=p.val(S["xnst"]), xnts=p.val(S["xnts"]))
    ar.reset()
    return res


def _phase_c(nc, p, S, ar, pb, xp_scr, xn2_scr, xn2tm_scr, wg1_scr, wg2_scr,
             wi_scr, wo_scr, stream_scr, gates_scr, o_scr, bgt, cvals, rsq,
             identb, identf, onesf, eiota, tokid, parange, thr256, tilest,
             out_h, B, LDC, dbgC):
    import concourse.bass as bass_mod
    IOX = bass_mod.IndirectOffsetOnAxis
    ar.mark()
    TTW = 1024
    NTT = NT // TTW
    NTILE = 15
    wbuf = ar.alloc(BF16, [3, 8 * HH])
    xc = ar.alloc(BF16, [8, TTW])
    xg_tm = ar.alloc(BF16, [2, 2, D])
    xg_fm = ar.alloc(BF16, [8, 256])
    hsb = ar.alloc(BF16, [2, 16, 256])
    o_stage = ar.alloc(F32, [2, D])
    ybuf = ar.alloc(F32, [3, D])
    wg1_sb = ar.alloc(BF16, [8, GDIM])
    wg2_sb = ar.alloc(BF16, [2, E])
    gsb = ar.alloc(BF16, [2, TTW])
    gexp2 = ar.alloc(F32, [2, 64])
    dC = ar.alloc(F32, [8, 8])
    t1t = ar.alloc(F32, [8, 1])
    t2t = ar.alloc(F32, [8, 1])
    grs2 = ar.alloc(F32, [2, 16])
    rrC2 = ar.alloc(F32, [2, 8])
    bnbC = ar.alloc(F32, [12])
    mvC = ar.alloc(F32, [2])
    sclC = ar.alloc(F32, [4])
    xpC = ar.alloc(F32, [2, D])
    outst = ar.alloc(F32, [2, D])
    mask_fm = ar.alloc(F32, [TTW])
    pos_fm = ar.alloc(F32, [TTW])
    zeros_fm = ar.alloc(F32, [TTW])
    pos_tm = ar.alloc(F32, [8, 8])
    segrow = ar.alloc(F32, [4, 16])
    segb = ar.alloc(F32, [1])
    eA8 = ar.alloc(F32, [8, 1])
    mA3 = ar.alloc(F32, [8, 8])
    mB3 = ar.alloc(F32, [8, 8])
    rt3 = ar.alloc(F32, [8, 8])
    posf = ar.alloc(F32, [16])
    gf = ar.alloc(F32, [16])
    rdiv = ar.alloc(F32, [3, 16])
    fABi = ar.alloc(I32, [16])
    posABi2 = ar.alloc(I32, [2, 16])
    widxi2 = ar.alloc(I32, [2, 16])
    tokidi = ar.alloc(I32, [8])
    widxf = ar.alloc(F32, [16])
    dbg_sb = ar.alloc(F32, [128])
    nc._dbg_ap = dbg_sb
    strm = ar.alloc(I32, [2, 30])
    gstrm = ar.alloc(F32, [2, 30])
    zstr = ar.alloc(I32, [30])
    zstrf = ar.alloc(F32, [30])

    dC3 = dC

    bw = [(S["semT"], B["T"]), (S["semV"], B["V"]), (S["semA"], B["A"]),
          (S["xpst0"], B["xpst0"]), (S["xpst1"], B["xpst1"]),
          (S["xnst"], B["xnst"]), (S["xnts"], B["xnts"])]

    p.emit("sync", lambda e: e.dma_start(
        out=wg1_sb, in_=wg1_scr.rearrange("p (k m) -> p k m", k=8, m=GDIM)),
        waits=bw + [(S["cwg"], 32)], inc=S["gld"], incby=16)
    p.emit("sync", lambda e: e.dma_start(
        out=wg2_sb, in_=wg2_scr.rearrange("p (k m) -> p k m", k=2, m=E)),
        waits=[(S["gld"], 16)], inc=S["gld"], incby=16)
    p.emit("vector", lambda e: e.memset(zeros_fm[0:8, :], 0.0),
           waits=bw, inc=S["semV"])
    p.emit("vector", lambda e: e.memset(zstr[:, :], 0), inc=S["semV"])
    p.emit("vector", lambda e: e.memset(zstrf[:, :], 0.0), inc=S["semV"])
    p.emit("vector", lambda e: e.tensor_copy(out=tokidi, in_=tokid),
           waits=[(S["ldc"], LDC["tokid"])], inc=S["semV"])
    vZ = p.val(S["semV"])
    vTokidi = [p.val(S["semV"])]

    xc_guard = 0
    pg_guard = 0
    pl_guard = 0
    gs_guard = 0
    gexp_guard = 0
    route_guard = 0
    xpc_guard = [0, 0]
    out_guard = [0, 0]
    lnC_guard = 0
    wslot_guard = [0, 0, 0]
    xg_guard = [0, 0]
    ha_g = [0, 0]
    ob_g = [0, 0, 0, 0]
    hsb_guard = [0, 0]
    ostage_g = {0: [0, 0], 1: [0, 0]}
    wg_wi = {}
    wg_wo = {}
    xgat_hist = [0, 0]
    ygat_hist = [0, 0]
    strz_hist = [0, 0]
    nL = 0

    def emit_route(tt):
        nonlocal pg_guard, gs_guard, pl_guard, route_guard, xc_guard
        par = tt % 2
        RS = {}
        xgat = S["xgat0"] if par == 0 else S["xgat1"]
        strld = S["strld0"] if par == 0 else S["strld1"]
        ostr = S["ostr0"] if par == 0 else S["ostr1"]

        p.emit("sync", lambda e, par=par: e.dma_start(
            out=stream_scr[par].rearrange("(p c) -> p c", p=128), in_=zstr),
            waits=bw + [(S["semV"], vZ), (S["ygat"], ygat_hist[par])],
            inc=S["strz"], incby=16)
        p.emit("sync", lambda e, par=par: e.dma_start(
            out=gates_scr[par].rearrange("(p c) -> p c", p=128), in_=zstrf),
            inc=S["strz"], incby=16)
        strz_v = p.val(S["strz"])

        p.emit("sync", lambda e, tt=tt: e.dma_start(
            out=xc, in_=xn2_scr[:, :, tt * TTW:(tt + 1) * TTW].rearrange(
                "f p t -> p f t")),
            waits=[(S["semT"], xc_guard)],
            inc=S["xcld0" if par == 0 else "xcld1"], incby=16)
        xcv = p.val(S["xcld0" if par == 0 else "xcld1"])
        xsem = S["xcld0" if par == 0 else "xcld1"]

        for m2 in range(2):
            for th2 in range(2):
                for k in range(8):
                    p.emit("tensor", lambda e, m2=m2, th2=th2, k=k: e.matmul(
                        pb[6][:, :], wg1_sb[:, k, 128 * m2:128 * (m2 + 1)],
                        xc[:, k, 512 * th2:512 * (th2 + 1)],
                        start=(k == 0), stop=(k == 7)),
                        waits=[(xsem, xcv), (S["gld"], 32), (S["semA"], pg_guard),
                               (S["semV"], route_guard)],
                        inc=S["semT"] if k == 7 else None)
                tG = p.val(S["semT"])
                p.emit("scalar", lambda e, m2=m2, th2=th2: e.activation(
                    out=gsb[:, m2, 512 * th2:512 * (th2 + 1)], in_=pb[6][:, :],
                    func=AF.Gelu, bias=bgt[:, m2:m2 + 1], scale=1.0),
                    waits=[(S["semT"], tG), (S["ldc"], LDC["bg1"]),
                           (S["semT"], gs_guard)],
                    inc=S["semA"])
                pg_guard = p.val(S["semA"])
        aG = p.val(S["semA"])
        for tokM in range(8):
            for k2 in range(2):
                p.emit("tensor", lambda e, tokM=tokM, k2=k2: e.matmul(
                    pb[7][:, 8 * tokM:8 * (tokM + 1)],
                    gsb[:, k2, 128 * tokM:128 * (tokM + 1)], wg2_sb[:, k2, :],
                    start=(k2 == 0), stop=(k2 == 1)),
                    waits=[(S["semA"], aG), (S["semA"], pl_guard),
                           (S["semV"], route_guard)],
                    inc=S["semT"] if (tokM == 7 and k2 == 1) else None)
        tLog = p.val(S["semT"])
        gs_guard = tLog
        xc_guard = tLog
        p.emit("scalar", lambda e: e.activation(
            out=gexp2[:, par, :], in_=pb[7][:, 0:64], func=AF.Exp, bias=cvals[:, 1:2], scale=1.0),
            waits=[(S["semT"], tLog), (S["semV"], gexp_guard),
                   (S["ygat"], ygat_hist[par])], inc=S["semA"])
        pl_guard = p.val(S["semA"])
        aExp = pl_guard
        p.emit("vector", lambda e: e.tensor_reduce(
            grs2[:, par, 0:8], gexp2[:, par, :].rearrange("p (a b) -> p a b", a=8, b=8), axis=AX.X, op=ALU.add),
            waits=[(S["semA"], aExp)], inc=S["semV"])
        vS = p.val(S["semV"])
        p.emit("vector", lambda e: e.reciprocal(grs2[:, par, 8:16], grs2[:, par, 0:8]),
               waits=[(S["semV"], vS)], inc=S["semV"])
        p.emit("vector", lambda e: e.tensor_reduce(
            t1t[:, :, 0], gexp2[:, par, :].rearrange("p (a b) -> p a b", a=8, b=8), axis=AX.X, op=ALU.max), inc=S["semV"])
        vT1 = p.val(S["semV"])
        p.emit("vector", lambda e: e.tensor_tensor(
            out=dC3, in0=gexp2[:, par, :].rearrange("p (a b) -> p a b", a=8, b=8), in1=t1t.broadcast_to([128, 8, 8]), op=ALU.is_ge),
            waits=[(S["semV"], vT1)], inc=S["semV"])
        vGe = p.val(S["semV"])
        p.emit("vector", lambda e: e.scalar_tensor_tensor(
            out=dC3, in0=dC3, scalar=-1e6, in1=gexp2[:, par, :].rearrange("p (a b) -> p a b", a=8, b=8), op0=ALU.mult, op1=ALU.add),
            waits=[(S["semV"], vGe)], inc=S["semV"])
        vD2 = p.val(S["semV"])
        p.emit("vector", lambda e: e.tensor_reduce(
            t2t[:, :, 0], dC3, axis=AX.X, op=ALU.max),
            waits=[(S["semV"], vD2)], inc=S["semV"])
        vT2 = p.val(S["semV"])
        p.emit("vector", lambda e: e.tensor_tensor(
            out=dC3, in0=gexp2[:, par, :].rearrange("p (a b) -> p a b", a=8, b=8), in1=t2t.broadcast_to([128, 8, 8]), op=ALU.is_ge),
            waits=[(S["semV"], vT2)], inc=S["semV"])
        vM2 = p.val(S["semV"])
        p.emit("vector", lambda e, par=par: e.tensor_mul(
            out=gexp2[:, par, :], in0=gexp2[:, par, :], in1=dC),
               waits=[(S["semV"], vM2)], inc=S["semV"])
        vGates = p.val(S["semV"])
        p.emit("vector", lambda e: e.tensor_tensor(
            out=rrC2[:, par, :], in0=grs2[:, par, 8:16], in1=grs2[:, par, 8:16], op=ALU.mult),
            inc=S["semV"])
        vRr = p.val(S["semV"])

        for tokM in range(8):
            bank = pb[6] if tokM < 4 else pb[7]
            p.emit("tensor", lambda e, tokM=tokM, bank=bank: e.transpose(
                bank[0:8, 128 * (tokM % 4):128 * (tokM % 4 + 1)],
                dC[:, tokM, :], identf[:, :]),
                waits=[(S["semV"], vGates), (S["semA"], aExp),
                       (S["ldc"], LDC["identf"])],
                inc=S["semT"] if tokM % 4 == 3 else None)
            if tokM == 3:
                t03 = p.val(S["semT"])
            if tokM == 7:
                t47 = p.val(S["semT"])
        p.emit("vector", lambda e: e.tensor_copy(
            out=mask_fm[0:8, 0:512], in_=pb[6][0:8, :]),
            waits=[(S["semT"], t03)], inc=S["semV"])
        p.emit("vector", lambda e: e.tensor_copy(
            out=mask_fm[0:8, 512:1024], in_=pb[7][0:8, :]),
            waits=[(S["semT"], t47)], inc=S["semV"])
        vMfm = p.val(S["semV"])
        p.emit("vector", lambda e: e.tensor_reduce(
            segrow[0:8, 0, 0:1],
            mask_fm[0:8, :], axis=AX.X, op=ALU.add),
            waits=[(S["semV"], vMfm)], inc=S["semV"])
        vCnt = p.val(S["semV"])
        p.emit("tensor", lambda e: e.transpose(
            pb[6][0:1, 0:8], segrow[0:8, 0, 0:1], identf[0:8, 0:8]),
            waits=[(S["semV"], vCnt)], inc=S["semT"])
        tCnt = p.val(S["semT"])
        p.emit("vector", lambda e: e.tensor_copy(
            out=segrow[0:1, 1, 0:8], in_=pb[6][0:1, 0:8]),
            waits=[(S["semT"], tCnt)], inc=S["semV"])
        vCrow = p.val(S["semV"])
        p.emit("vector", lambda e: e.tensor_tensor(
            out=segrow[0:1, 2, 0:8].rearrange("p a -> p a 1").broadcast_to([1, 8, 4])
            if False else segrow[0:1, 3, 0:8],
            in0=segrow[0:1, 1, 0:8], in1=segrow[0:1, 1, 0:8], op=ALU.bypass),
            waits=[(S["semV"], vCrow)], inc=S["semV"]) if False else None
        p.emit("vector", lambda e: e.tensor_single_scalar(
            out=segrow[0:1, 2, 0:8], in_=segrow[0:1, 1, 0:8], scalar=0.5, op=ALU.is_gt),
            waits=[(S["semV"], vCrow)], inc=S["semV"])
        for jthr in (256.0, 512.0, 768.0):
            vv = p.val(S["semV"])
            p.emit("vector", lambda e, jthr=jthr: e.scalar_tensor_tensor(
                out=segrow[0:1, 2, 0:8], in0=segrow[0:1, 1, 0:8], scalar=jthr,
                in1=segrow[0:1, 2, 0:8], op0=ALU.is_gt, op1=ALU.add),
                waits=[(S["semV"], vv)], inc=S["semV"])
        vCeil = p.val(S["semV"])
        p.emit("vector", lambda e: e.tensor_single_scalar(
            out=segrow[0:1, 2, 0:8], in_=segrow[0:1, 2, 0:8], scalar=256.0,
            op=ALU.mult),
            waits=[(S["semV"], vCeil)], inc=S["semV"])
        vSz = p.val(S["semV"])
        p.emit("vector", lambda e: e.tensor_tensor_scan(
            out=segrow[0:1, 3, 0:8], data0=segrow[0:1, 2, 0:8],
            data1=zeros_fm[0:1, 0:8], initial=0.0, op0=ALU.add, op1=ALU.add),
            waits=[(S["semV"], vSz)], inc=S["semV"])
        vEnd = p.val(S["semV"])
        p.emit("vector", lambda e: e.tensor_sub(
            out=segrow[0:1, 0, 0:8], in0=segrow[0:1, 3, 0:8],
            in1=segrow[0:1, 2, 0:8]),
            waits=[(S["semV"], vEnd)], inc=S["semV"])
        vBase = p.val(S["semV"])
        p.emit("tensor", lambda e: e.transpose(
            pb[6][0:8, 8:9], segrow[0:1, 0, 0:8], identf[0:1, 0:1]),
            waits=[(S["semV"], vBase)], inc=S["semT"])
        tBase = p.val(S["semT"])
        p.emit("vector", lambda e: e.tensor_copy(out=segb[0:8, :], in_=pb[6][0:8, 8:9]),
               waits=[(S["semT"], tBase)], inc=S["semV"])
        vSegb = p.val(S["semV"])
        p.emit("vector", lambda e: e.tensor_tensor(
            out=widxf[0:1, 0:15].rearrange("p a -> p 1 a").broadcast_to([1, 8, 15])
            if False else widxf[0:1, 0:15],
            in0=widxf[0:1, 0:15], in1=widxf[0:1, 0:15], op=ALU.bypass),
            inc=None) if False else None
        p.emit("vector", lambda e: e.memset(widxf[0:1, 0:15], 0.0),
               waits=[(S["semV"], p.val(S["semV"]))], inc=S["semV"])
        for ee in range(8):
            vv = p.val(S["semV"])
            p.emit("vector", lambda e, ee=ee: e.scalar_tensor_tensor(
                out=widxf[0:1, 0:15], in0=tilest[0:1, 0:15],
                scalar=segrow[0:1, 3, ee:ee + 1], in1=widxf[0:1, 0:15],
                op0=ALU.is_ge, op1=ALU.add),
                waits=[(vv and S["semV"] or S["semV"], vv), (S["semV"], vEnd),
                       (S["ldc"], LDC["tilest"])], inc=S["semV"])
        vTe = p.val(S["semV"])
        p.emit("vector", lambda e: e.tensor_single_scalar(
            out=widxf[0:1, 0:15], in_=widxf[0:1, 0:15], scalar=7.0, op=ALU.min),
            waits=[(S["semV"], vTe)], inc=S["semV"])
        vTe2 = p.val(S["semV"])
        p.emit("tensor", lambda e: e.matmul(
            pb[7][:, 0:15], onesf[0:1, :], widxf[0:1, 0:15], start=True, stop=True),
            waits=[(S["semV"], vTe2), (S["ldc"], LDC["onesf"])], inc=S["semT"])
        tWx = p.val(S["semT"])
        p.emit("vector", lambda e: e.scalar_tensor_tensor(
            out=widxf[:, 0:15], in0=pb[7][:, 0:15], scalar=128.0,
            in1=parange.broadcast_to([128, 15]), op0=ALU.mult, op1=ALU.add),
            waits=[(S["semT"], tWx), (S["ldc"], LDC["parange"])], inc=S["semV"])
        vWx = p.val(S["semV"])
        p.emit("vector", lambda e, par=par: e.tensor_copy(
            out=widxi2[:, par, 0:15], in_=widxf[:, 0:15]),
               waits=[(S["semV"], vWx), (S["wgat"], p.val(S["wgat"]))],
               inc=S["semV"])
        vWxi = p.val(S["semV"])
        p.emit("vector", lambda e: e.tensor_tensor_scan(
            out=pos_fm[0:8, :], data0=mask_fm[0:8, :], data1=zeros_fm[0:8, :],
            initial=0.0, op0=ALU.add, op1=ALU.add),
            waits=[(S["semV"], vMfm)], inc=S["semV"])
        vScan = p.val(S["semV"])
        p.emit("vector", lambda e: e.tensor_scalar(
            out=pos_fm[0:8, :], in0=pos_fm[0:8, :], scalar1=segb[0:8, 0:1],
            scalar2=-1.0, op0=ALU.add, op1=ALU.add),
            waits=[(S["semV"], vScan), (S["semV"], vSegb)], inc=S["semV"])
        vPos = p.val(S["semV"])
        for tokM in range(8):
            bank = pb[6]
            p.emit("tensor", lambda e, tokM=tokM: e.transpose(
                pb[6][:, 8 * tokM:8 * (tokM + 1)],
                pos_fm[0:8, 128 * tokM:128 * (tokM + 1)], identf[0:8, 0:8]),
                waits=[(S["semV"], vPos), (S["semV"], vSegb)],
                inc=S["semT"] if tokM == 7 else None)
        tPtm = p.val(S["semT"])
        p.emit("vector", lambda e: e.tensor_copy(
            out=pos_tm[:, :, :], in_=pb[6][:, 0:64].rearrange(
                "p (a b) -> p a b", a=8, b=8)),
            waits=[(S["semT"], tPtm)], inc=S["semV"])
        vPtm = p.val(S["semV"])

        eiota3 = eiota.unsqueeze(1).broadcast_to([128, 8, 8])
        scat_prev = p.val(S["scat"])
        ygat_prev = p.val(S["ygat"])
        p.emit("vector", lambda e: e.scalar_tensor_tensor(
            out=rt3, in0=dC3, scalar=10.0, in1=eiota3, op0=ALU.mult,
            op1=ALU.subtract),
            waits=[(S["semV"], vPtm), (S["ldc"], LDC["eiota"]),
                   (S["scat"], scat_prev), (S["ygat"], ygat_prev)],
            inc=S["semV"])
        p.emit("vector", lambda e: e.tensor_reduce(
            eA8, rt3, axis=AX.X, op=ALU.max),
            waits=[(S["semV"], p.val(S["semV"]))], inc=S["semV"])
        p.emit("vector", lambda e: e.tensor_scalar(
            out=eA8, in0=eA8, scalar1=-1.0, scalar2=10.0, op0=ALU.mult, op1=ALU.add),
            waits=[(S["semV"], p.val(S["semV"]))], inc=S["semV"])
        p.emit("vector", lambda e: e.tensor_tensor(
            out=mA3, in0=eiota3, in1=eA8.broadcast_to([128, 8, 8]), op=ALU.is_equal),
            waits=[(S["semV"], p.val(S["semV"]))], inc=S["semV"])
        p.emit("vector", lambda e: e.tensor_sub(out=mB3, in0=dC3, in1=mA3),
               waits=[(S["semV"], p.val(S["semV"]))], inc=S["semV"])
        for (msk3, base) in ((mA3, 0), (mB3, 8)):
            p.emit("vector", lambda e, msk3=msk3: e.tensor_mul(
                out=rt3, in0=pos_tm, in1=msk3),
                waits=[(S["semV"], p.val(S["semV"]))], inc=S["semV"])
            p.emit("vector", lambda e, base=base: e.tensor_reduce(
                posf[:, base:base + 8], rt3, axis=AX.X, op=ALU.add),
                waits=[(S["semV"], p.val(S["semV"]))], inc=S["semV"])
            p.emit("vector", lambda e, msk3=msk3: e.tensor_mul(
                out=rt3, in0=gexp2[:, par, :].rearrange("p (a b) -> p a b", a=8, b=8), in1=msk3),
                waits=[(S["semV"], p.val(S["semV"]))], inc=S["semV"])
            p.emit("vector", lambda e, base=base: e.tensor_reduce(
                gf[:, base:base + 8], rt3, axis=AX.X, op=ALU.add),
                waits=[(S["semV"], p.val(S["semV"]))], inc=S["semV"])
        p.emit("vector", lambda e: e.tensor_copy(out=posABi2[:, par, :], in_=posf),
               waits=[(S["semV"], p.val(S["semV"]))], inc=S["semV"])
        p.emit("vector", lambda e: e.tensor_copy(out=rdiv[:, 2, :], in_=posf),
               inc=S["semV"])
        p.emit("vector", lambda e: e.memset(rdiv[:, 0, :], 0.0), inc=S["semV"])
        for sub in (2048.0, 1024.0, 512.0, 256.0, 128.0):
            p.emit("vector", lambda e, sub=sub: e.tensor_single_scalar(
                out=rdiv[:, 1, :], in_=rdiv[:, 2, :], scalar=sub - 0.5, op=ALU.is_gt),
                waits=[(S["semV"], p.val(S["semV"]))], inc=S["semV"])
            p.emit("vector", lambda e, sub=sub: e.scalar_tensor_tensor(
                out=rdiv[:, 2, :], in0=rdiv[:, 1, :], scalar=-sub, in1=rdiv[:, 2, :],
                op0=ALU.mult, op1=ALU.add),
                waits=[(S["semV"], p.val(S["semV"]))], inc=S["semV"])
            p.emit("vector", lambda e, sub=sub: e.scalar_tensor_tensor(
                out=rdiv[:, 0, :], in0=rdiv[:, 1, :], scalar=sub / 128.0,
                in1=rdiv[:, 0, :], op0=ALU.mult, op1=ALU.add),
                waits=[(S["semV"], p.val(S["semV"]))], inc=S["semV"])
        p.emit("vector", lambda e: e.scalar_tensor_tensor(
            out=rdiv[:, 1, :], in0=rdiv[:, 2, :], scalar=30.0, in1=rdiv[:, 0, :],
            op0=ALU.mult, op1=ALU.add),
            waits=[(S["semV"], p.val(S["semV"]))], inc=S["semV"])
        p.emit("vector", lambda e: e.tensor_copy(out=fABi, in_=rdiv[:, 1, :]),
               waits=[(S["semV"], p.val(S["semV"]))], inc=S["semV"])
        vPrep = p.val(S["semV"])
        for tokM in range(8):
            for (slot_i, gbase) in ((0, 0), (1, 8)):
                col = slot_i * 8 + tokM
                p.emit("gpsimd", lambda e, par=par, col=col, tokM=tokM:
                       e.indirect_dma_start(
                    out=stream_scr[par].unsqueeze(0),
                    out_offset=IOX(fABi[:, col:col + 1], 1),
                    in_=tokidi[:, tokM:tokM + 1], in_offset=None),
                    waits=[(S["semV"], vPrep), (S["strz"], strz_v),
                           (S["semV"], vTokidi[0])],
                    inc=S["scat"], incby=16)
                p.emit("gpsimd", lambda e, par=par, col=col: e.indirect_dma_start(
                    out=gates_scr[par].unsqueeze(0),
                    out_offset=IOX(fABi[:, col:col + 1], 1),
                    in_=gf[:, col:col + 1], in_offset=None),
                    inc=S["scat"], incby=16)
        scat_all = p.val(S["scat"])
        p.emit("sync", lambda e, par=par: e.dma_start(
            out=strm[:, par, :],
            in_=stream_scr[par].rearrange("(p c) -> p c", p=128)),
            waits=[(S["scat"], scat_all),
                   (S["xgat0"] if par == 0 else S["xgat1"], xgat_hist[par])],
            inc=strld, incby=16)
        p.emit("sync", lambda e, par=par: e.dma_start(
            out=gstrm[:, par, :],
            in_=gates_scr[par].rearrange("(p c) -> p c", p=128)),
            inc=strld, incby=16)
        strld_v = p.val(strld)

        route_guard = p.val(S["semV"])
        RS.update(dict(par=par, xgat=xgat, strld=strld, ostr=ostr,
                       strld_v=strld_v, vGates=vGates, vRr=vRr, vWxi=vWxi,
                       xcv=xcv, xsem=xsem))
        return RS

    route_states = {}
    pend_epi = None
    ostr_hist = {0: 0, 1: 0}
    yln_hist = {0: 0, 1: 0, 2: 0}

    ypart = {}

    def emit_epi_g1(par, ostr_all, g):
        yso = g % 3
        yg0 = p.val(S["ygat1"])
        p.emit("gpsimd", lambda e, g=g, yso=yso, par=par: e.indirect_dma_start(
            out=ybuf[:, yso, :], out_offset=None,
            in_=o_scr[par], in_offset=IOX(posABi2[:, par, g:g + 1], 0)),
            waits=[(S["ostr0" if par == 0 else "ostr1"], ostr_all),
                   (S["ygat1"], yg0), (S["semA"], yln_hist[yso])],
            inc=S["ygat1"], incby=16)
        ypart[g] = p.val(S["ygat1"])

    def emit_epi_g2(tt, par, g):
        nonlocal lnC_guard
        so = g % 2
        yso = g % 3
        xpsem = S["xpldC0"] if so == 0 else S["xpldC1"]
        otsem = S["outst0"] if so == 0 else S["outst1"]
        yg1 = ypart[g]
        p.emit("gpsimd", lambda e, g=g, yso=yso, par=par: e.indirect_dma_start(
            out=ybuf[:, yso, :], out_offset=None,
            in_=o_scr[par], in_offset=IOX(posABi2[:, par, 8 + g:9 + g], 0),
            compute_op=ALU.add),
            waits=[(S["ygat1"], yg1), (S["ygat"], p.val(S["ygat"]))],
            inc=S["ygat"], incby=16)
        ygv = p.val(S["ygat"])
        p.emit("sync", lambda e, tt=tt, g=g, so=so: e.dma_start(
            out=xpC[:, so, :],
            in_=xp_scr[tt * TTW + 128 * g: tt * TTW + 128 * (g + 1), :]),
            waits=[(S["semV"], xpc_guard[so])], inc=xpsem, incby=16)
        xpv = p.val(xpsem)
        p.emit("vector", lambda e, g=g, yso=yso: e.bn_stats(
            bnbC[:, 0:6], ybuf[:, yso, 0:512]),
            waits=[(S["ygat"], ygv)])
        p.emit("vector", lambda e, g=g, yso=yso: e.bn_stats(
            bnbC[:, 6:12], ybuf[:, yso, 512:1024]), inc=S["semV"])
        vbn = p.val(S["semV"])
        p.emit("vector", lambda e: e.bn_aggr(mvC, bnbC),
               waits=[(S["semV"], vbn)], inc=S["semV"])
        vmv = p.val(S["semV"])
        p.emit("vector", lambda e, g=g, par=par: e.tensor_tensor(
            out=sclC[:, 0:1], in0=mvC[:, 1:2], in1=rrC2[:, par, g:g + 1],
            op=ALU.mult),
            waits=[(S["semV"], vmv), (S["semV"], vRr)], inc=S["semV"])
        vvsc = p.val(S["semV"])
        p.emit("scalar", lambda e: e.activation(
            out=sclC[:, 1:2], in_=sclC[:, 0:1], func=AF.Ln,
            bias=cvals[:, 0:1], scale=1.0),
            waits=[(S["semV"], vvsc)], inc=S["semA"])
        p.emit("scalar", lambda e: e.activation(
            out=sclC[:, 1:2], in_=sclC[:, 1:2], func=AF.Exp, bias=0.0, scale=-0.5),
            waits=[(S["semA"], p.val(S["semA"]))], inc=S["semA"])
        aRstd3 = p.val(S["semA"])
        p.emit("vector", lambda e, g=g, par=par: e.tensor_tensor(
            out=sclC[:, 2:3], in0=sclC[:, 1:2], in1=grs2[:, par, 8 + g:9 + g],
            op=ALU.mult),
            waits=[(S["semA"], aRstd3)], inc=S["semV"])
        vS3 = p.val(S["semV"])
        p.emit("vector", lambda e: e.scalar_tensor_tensor(
            out=sclC[:, 3:4], in0=mvC[:, 0:1], scalar=-1.0, in1=sclC[:, 2:3],
            op0=ALU.mult, op1=ALU.mult),
            waits=[(S["semV"], vS3)], inc=S["semV"])
        vB3 = p.val(S["semV"])
        p.emit("scalar", lambda e, g=g, so=so, yso=yso: e.activation(
            out=outst[:, so, :], in_=ybuf[:, yso, :], func=AF.Identity,
            bias=sclC[:, 3:4], scale=sclC[:, 2:3]),
            waits=[(S["semV"], vB3), (otsem, out_guard[so])], inc=S["semA"])
        lnC_guard = p.val(S["semA"])
        yln_hist[yso] = lnC_guard
        aLn3 = lnC_guard
        p.emit("vector", lambda e, so=so: e.tensor_add(
            out=outst[:, so, :], in0=outst[:, so, :], in1=xpC[:, so, :]),
            waits=[(S["semA"], aLn3), (xpsem, xpv)], inc=S["semV"])
        vOut = p.val(S["semV"])
        xpc_guard[so] = vOut
        p.emit("sync", lambda e, tt=tt, g=g, so=so: e.dma_start(
            out=out_h[tt * TTW + 128 * g: tt * TTW + 128 * (g + 1), :],
            in_=outst[:, so, :]),
            waits=[(S["semV"], vOut)], inc=otsem, incby=16)
        out_guard[so] = p.val(otsem)
        if g == 7:
            ygat_hist[par] = p.val(S["ygat"])

    for tt in range(NTT):
        if tt == 0:
            route_states[0] = emit_route(0)
        RS = route_states[tt]
        par = RS["par"]
        xgat = RS["xgat"]
        strld = RS["strld"]
        ostr = RS["ostr"]
        strld_v = RS["strld_v"]
        vGates = RS["vGates"]
        vRr = RS["vRr"]
        vWxi = RS["vWxi"]
        def emit_wgather(j, tt=tt, parts=(0, 1)):
            nonlocal nL
            for (which, scr) in [(0, wi_scr), (1, wo_scr)][parts[0]:parts[-1] + 1]:
                slot = nL % 3
                g0 = p.val(S["wgat"])
                p.emit("gpsimd", lambda e, j=j, which=which, scr=scr, slot=slot,
                       par=par: e.indirect_dma_start(
                    out=wbuf[:, slot, :], out_offset=None,
                    in_=scr.rearrange("e p w -> (e p) w"),
                    in_offset=IOX(widxi2[:, par, j:j + 1], 0)),
                    waits=[(S["semV"], vWxi), (S["wgat"], g0),
                           (S["semT"], wslot_guard[slot]),
                           (S["cwin"], 16 * E), (S["cwout"], 16 * E)],
                    inc=S["wgat"], incby=16)
                nL += 1
                if which == 0:
                    wg_wi[(tt, j)] = (16 * nL, slot)
                else:
                    wg_wo[(tt, j)] = (16 * nL, slot)

        def emit_xgather(j, tt=tt, par=par, xgat=xgat, strld_v=strld_v):
            slot = j % 2
            x0 = p.val(xgat)
            for i in range(2):
                def _dbg(e, j=j, i=i, slot=slot, tt=tt, par=par):
                    try:
                        return e.indirect_dma_start(
                            out=xg_tm[:, slot, i, :], out_offset=None,
                            in_=xn2tm_scr,
                            in_offset=IOX(strm[:, par, :][:, 2 * j + i:2 * j + i + 1], 0),
                            element_offset=tt * TTW * D)
                    except Exception:
                        print("XGFAIL tt", tt, "j", j, "i", i, "slot", slot, "par", par)
                        print("  out:", xg_tm[:, slot, i, :])
                        print("  idx:", strm[:, par, :][:, 2 * j + i:2 * j + i + 1])
                        raise
                p.emit("gpsimd", lambda e, f=_dbg: f(e),
                    waits=[(strld, strld_v), (xgat, x0),
                           (S["semT"], xg_guard[slot]),
                           (S["xnts"], B["xnts"])],
                    inc=xgat, incby=16)
                x0 = p.val(xgat)


        emit_wgather(0)
        emit_xgather(0)
        h_end = [0] * NTILE
        o_end = [0] * NTILE
        for j in range(NTILE):
            if j == 10 and tt + 1 < NTT:
                route_states[tt + 1] = emit_route(tt + 1)
            slot = j % 2
            xgv = p.val(xgat)
            wgv = 16 * nL
            wg_wi_cnt, wslot_wi = wg_wi[(tt, j)]
            wg_wo_cnt, wslot_wo = wg_wo[(tt, j)]
            wi_v = wbuf[:, wslot_wi, :].rearrange("p (a b) -> p a b", a=8, b=HH)
            wo_v = wbuf[:, wslot_wo, :].rearrange("p (a b) -> p a b", a=16, b=D)
            if j + 1 < NTILE:
                emit_wgather(j + 1, parts=(0, 0))
                emit_xgather(j + 1)
            for i in range(2):
                bank = pb[i]
                bv = bf16_bank(bank).rearrange("p (f c) -> p f c", f=8, c=128)
                for f in range(8):
                    p.emit("tensor", lambda e, i=i, f=f, slot=slot, bv=bv: e.transpose(
                        bv[:, f, :], xg_tm[:, slot, i, 128 * f:128 * (f + 1)], identb),
                        waits=[(xgat, xgv), (S["semA"], ha_g[i]),
                               (S["ldc"], LDC["identb"])],
                        inc=S["semT"] if f == 7 else None)
                tT = p.val(S["semT"])
                p.emit("scalar", lambda e, i=i, bv=bv: e.copy(
                    out=xg_fm[:, :, :, 128 * i:128 * (i + 1)].squeeze(1)
                    if False else xg_fm[:, :, 128 * i:128 * (i + 1)],
                    in_=bv[:, :, :]),
                    waits=[(S["semT"], tT)], inc=S["semA"])
                ha_g[i] = p.val(S["semA"])
            aXgT = p.val(S["semA"])
            xg_guard[slot] = p.val(S["semT"])
            gelu_last = 0
            for m in range(16):
                bank = pb[m % 2]
                for k in range(8):
                    p.emit("tensor", lambda e, m=m, k=k, bank=bank, wi_v=wi_v:
                           e.matmul(
                        bank[:, 0:256], wi_v[:, k, 128 * m:128 * (m + 1)],
                        xg_fm[:, k, :], start=(k == 0), stop=(k == 7)),
                        waits=[(S["wgat"], wg_wi_cnt), (S["semA"], aXgT),
                               (S["semA"], ha_g[m % 2])],
                        inc=S["semT"] if k == 7 else None)
                tH = p.val(S["semT"])
                p.emit("scalar", lambda e, m=m, bank=bank, j=j: e.activation(
                    out=hsb[:, j % 2, m, :], in_=bank[:, 0:256], func=AF.Gelu,
                    bias=0.0, scale=1.0),
                    waits=[(S["semT"], tH), (S["semT"], hsb_guard[j % 2])],
                    inc=S["semA"])
                ha_g[m % 2] = p.val(S["semA"])
                gelu_last = ha_g[m % 2]
            h_end[j] = p.val(S["semT"])
            wslot_guard[wslot_wi] = h_end[j]
            if j + 1 < NTILE:
                emit_wgather(j + 1, parts=(1, 1))
            for i in range(2):
                for dn in range(2):
                    bi = 2 + 2 * i + dn
                    bank = pb[bi]
                    for k in range(16):
                        p.emit("tensor", lambda e, i=i, dn=dn, k=k, bank=bank,
                               wo_v=wo_v, j=j: e.matmul(
                            bank[:, :], hsb[:, j % 2, k, 128 * i:128 * (i + 1)],
                            wo_v[:, k, 512 * dn:512 * (dn + 1)],
                            start=(k == 0), stop=(k == 15)),
                            waits=[(S["semA"], gelu_last),
                                   (S["wgat"], wg_wo_cnt),
                                   (S["semV"], ob_g[bi - 2])],
                            inc=S["semT"] if k == 15 else None)
                    tO = p.val(S["semT"])
                    gcol = gstrm[:, par, 2 * j + i:2 * j + i + 1]
                    p.emit("vector", lambda e, bank=bank, i=i, dn=dn, gcol=gcol:
                           e.tensor_scalar(
                        out=o_stage[:, i, 512 * dn:512 * (dn + 1)], in0=bank[:, :],
                        scalar1=gcol, scalar2=None, op0=ALU.mult),
                        waits=[(S["semT"], tO), (strld, strld_v),
                               (ostr, ostage_g[par][i]),
                               (S["ostr1" if par == 0 else "ostr0"],
                                ostr_hist[1 - par])],
                        inc=S["semV"])
                    ob_g[bi - 2] = p.val(S["semV"])
                vEv = p.val(S["semV"])
                p.emit("sync", lambda e, j=j, i=i, par=par: e.dma_start(
                    out=o_scr[par][128 * (2 * j + i):128 * (2 * j + i + 1), :],
                    in_=o_stage[:, i, :]),
                    waits=[(S["semV"], vEv), (S["ygat"], ygat_hist[par]),
                           (ostr, p.val(ostr))],
                    inc=ostr, incby=16)
                ostage_g[par][i] = p.val(ostr)
            o_end[j] = p.val(S["semT"])
            hsb_guard[j % 2] = o_end[j]
            wslot_guard[wslot_wo] = o_end[j]
            if pend_epi is not None and 1 <= j <= 8:
                emit_epi_g1(pend_epi[1], pend_epi[2], g=j - 1)
            if pend_epi is not None and 2 <= j:
                emit_epi_g2(pend_epi[0], pend_epi[1], g=j - 2)
                if j == 9:
                    pend_epi = None
        xgat_hist[par] = p.val(xgat)
        ostr_all = p.val(ostr)
        ostr_hist[par] = ostr_all
        pend_epi = (tt, par, ostr_all)
        gexp_guard = p.val(S["semV"])
        route_guard = p.val(S["semV"])

    if pend_epi is not None:
        for g in range(8):
            emit_epi_g1(pend_epi[1], pend_epi[2], g=g)
            emit_epi_g2(pend_epi[0], pend_epi[1], g=g)
        pend_epi = None

    res = dict(outst=p.val(S["outst0"]) + p.val(S["outst1"]))
    ar.reset()
    return res


_consts = None


def _make_consts():
    global _consts
    if _consts is None:
        identf = np.eye(128, dtype=np.float32)
        masks = np.zeros((128, 4 * QW), dtype=np.float32)
        k = np.arange(128)[:, None]
        q = np.arange(QW)[None, :]
        for d in range(4):
            masks[:, d * QW:(d + 1) * QW] = np.where(q >= k + 128 * d, 0.0, -30000.0)
        pcol = np.arange(128, dtype=np.float32)[:, None]
        _consts = dict(
            identf=identf,
            identb=identf.astype(ml_dtypes.bfloat16),
            ones16=np.ones((128, 128), dtype=ml_dtypes.bfloat16),
            onesf=np.ones((128, 128), dtype=np.float32),
            masks=masks.astype(ml_dtypes.bfloat16),
            eiota=np.broadcast_to(np.arange(8, dtype=np.float32), (128, 8)).copy(),
            tokid=(pcol + 128.0 * np.arange(8, dtype=np.float32)[None, :]
                   ).astype(np.float32),
            parange=pcol.astype(np.float32).copy(),
            thr256=np.broadcast_to(np.array([0.0, 256.0, 512.0, 768.0], np.float32),
                                   (128, 4)).copy(),
            tilest=np.broadcast_to(256.0 * np.arange(16, dtype=np.float32),
                                   (128, 16)).copy(),
        )
    return _consts


def make_in_maps(L0_attn_x, L0_attn_W_qkv, L0_ffn_wg1, L0_ffn_bg1, L0_ffn_wg2,
                 L0_ffn_w_in, L0_ffn_w_out):
    x = np.ascontiguousarray(np.asarray(L0_attn_x, dtype=np.float32))
    base = dict(
        wqkv=np.ascontiguousarray(np.asarray(L0_attn_W_qkv, np.float32)),
        wg1=np.ascontiguousarray(np.asarray(L0_ffn_wg1, np.float32)),
        bg1=np.ascontiguousarray(np.asarray(L0_ffn_bg1, np.float32)),
        wg2=np.ascontiguousarray(np.asarray(L0_ffn_wg2, np.float32)),
        w_in=np.ascontiguousarray(np.asarray(L0_ffn_w_in, np.float32)),
        w_out=np.ascontiguousarray(np.asarray(L0_ffn_w_out, np.float32)),
        **_make_consts(),
    )
    return [dict(base, x=x[cc]) for cc in range(8)]


def kernel(L0_attn_x, L0_attn_W_qkv, L0_ffn_wg1, L0_ffn_bg1, L0_ffn_wg2,
           L0_ffn_w_in, L0_ffn_w_out):
    in_maps = make_in_maps(L0_attn_x, L0_attn_W_qkv, L0_ffn_wg1, L0_ffn_bg1,
                           L0_ffn_wg2, L0_ffn_w_in, L0_ffn_w_out)
    nc = build_program(upto="C", debug=False)
    res = run_bass_kernel_spmd(nc, in_maps, core_ids=list(range(8)))
    return np.stack([np.asarray(r["out"], np.float32) for r in res.results], axis=0)
